# version 03
# baseline (speedup 1.0000x reference)
import numpy as np
from contextlib import ExitStack

import concourse.bass as bass
import concourse.mybir as mybir
from concourse.bass_utils import run_bass_kernel_spmd

F32 = mybir.dt.float32
F32R = mybir.dt.float32r
AF = mybir.ActivationFunctionType
ALU = mybir.AluOpType
AX = mybir.AxisListType.X

DM = 2048
S = 8192
TOK = 1024
NCH = 16
STRIP = 512
NSTRIP = 16
HPG = 4
D1 = 65
NE = 16
FMOE = 1024
FSH = 2048
NTILE = 8

ENGINES = ("sync", "gpsimd", "tensor", "vector", "scalar")


class Sem:
    __slots__ = ("name", "v", "h")

    def __init__(self, name):
        self.name = name
        self.v = 0
        self.h = None


class Prog:
    def __init__(self):
        self.ops = []
        self.sems = {}
        self.serialize = False

    def sem(self, name):
        s = self.sems.get(name)
        if s is None:
            s = Sem(name)
            self.sems[name] = s
        return s

    def op(self, engine, fn, waits=(), inc=None, inc_n=1):
        if self.serialize:
            waits = list(waits) + [(s, s.v) for s in self.sems.values()]
        self.ops.append((engine, fn, tuple(waits), inc, inc_n))
        if inc is not None:
            inc.v += inc_n
        return inc.v if inc is not None else None

    def emit(self, nc, block, sx):
        for s in self.sems.values():
            s.h = sx.enter_context(nc.semaphore("sem_" + s.name))
        watermark = {e: {} for e in ENGINES}
        by_engine = {e: [] for e in ENGINES}
        for rec in self.ops:
            by_engine[rec[0]].append(rec)

        def emitter(engine_name):
            def run(eng):
                wm = watermark[engine_name]
                for _, fn, waits, inc, inc_n in by_engine[engine_name]:
                    for (s, v) in waits:
                        if wm.get(s.name, -1) >= v:
                            continue
                        eng.wait_ge(s.h, v)
                        wm[s.name] = v
                    if fn is None:
                        continue
                    inst = fn(eng)
                    if inc is not None:
                        inst.then_inc(inc.h, inc_n)
            return run

        block.sync(emitter("sync"))
        block.gpsimd(emitter("gpsimd"))
        block.tensor(emitter("tensor"))
        block.vector(emitter("vector"))
        block.scalar(emitter("scalar"))


DEBUG_TAPS = False


SERIALIZE = False
TAP_SVAR = True


def build_program():
    nc = bass.Bass("TRN2", target_bir_lowering=False, debug=False)
    P = Prog()
    P.serialize = SERIALIZE

    din = lambda n, s, d: nc.dram_tensor(n, s, d, kind="ExternalInput").ap()
    x_own = din("x_own", [TOK, DM], F32)
    w_in_gT = din("w_in_gT", [DM, 384], F32)
    rmsw_row = din("rmsw_row", [128, DM], F32)
    ropeq = din("ropeq", [128, S], F32)
    ropek = din("ropek", [64, S], F32)
    ident_r = din("ident_r", [128, 128], F32R)
    ident_f = din("ident_f", [128, 128], F32)
    dmasks = din("dmasks", [128, 4 * STRIP], F32R)
    gwTd = din("gwT", [DM, NE], F32)
    ebias = din("ebias", [128, NE], F32)
    selc = din("selc", [NE, NE * 128], F32R)
    gateT = din("gateT", [NE, DM, FMOE], F32)
    upT = din("upT", [NE, DM, FMOE], F32)
    downT = din("downT", [NE, FMOE, DM], F32)
    shgT = din("shgT", [DM, FSH], F32)
    shuT = din("shuT", [DM, FSH], F32)
    shdT = din("shdT", [FSH, DM], F32)
    w_og = din("w_og", [256, DM], F32)

    out = nc.dram_tensor("out", [TOK, DM], F32, kind="ExternalOutput").ap()
    dbg = {}
    if DEBUG_TAPS:
        for nm, shp, dt_ in [("d_z", [DM, TOK], F32R), ("d_xn", [DM, TOK], F32R),
                             ("d_q", [4 * D1, S], F32R), ("d_K", [D1, S], F32R),
                             ("d_V", [128, 64 * D1], F32R), ("d_oloc", [128, 2 * S], F32R),
                             ("d_yp", [S, DM], F32R), ("d_yo", [TOK, DM], F32R),
                             ("d_g", [TOK, DM], F32), ("d_W", [128, 64], F32),
                             ("d_act", [128, 16 * 512], F32R),
                             ("d_mc", [128, 64], F32), ("d_rc", [128, 64], F32),
                             ("d_xnt", [128, DM], F32R), ("d_zt", [128, DM], F32R),
                             ("d_cc", [128, 64], F32), ("d_routed", [128, 4 * DM], F32)]:
            dbg[nm] = nc.dram_tensor(nm, shp, dt_, kind="ExternalOutput").ap()

    z_contrib = nc.dram_tensor("z_contrib", [DM, TOK], F32R)
    z_all = nc.dram_tensor("z_all", [8 * DM, TOK], F32R)
    xn_dram = nc.dram_tensor("xn_dram", [DM, TOK], F32R)
    q_dram = nc.dram_tensor("q_dram", [4 * D1, S], F32R)
    y_part = nc.dram_tensor("y_part", [S, DM], F32R)
    y_own = nc.dram_tensor("y_own", [TOK, DM], F32R)
    g_dram = nc.dram_tensor("g_dram", [TOK, DM], F32)

    CX = ExitStack()
    sbuf = lambda n, s, d: CX.enter_context(nc.sbuf_tensor(n, s, d))

    id_r = sbuf("id_r", [128, 128], F32R)
    id_f = sbuf("id_f", [128, 128], F32)
    ones_row = sbuf("ones_row", [1, 128], F32R)
    eps5 = sbuf("eps5", [128, 1], F32)
    eps6 = sbuf("eps6", [128, 1], F32)
    mcol = sbuf("mcol", [128, 64], F32)
    rcol = sbuf("rcol", [128, 64], F32)

    ld = P.sem("ld")
    gl = P.sem("gl")
    cc1 = P.sem("cc1")
    cc2 = P.sem("cc2")
    vv = P.sem("vv")
    sc = P.sem("sc")
    te = P.sem("te")
    pp = P.sem("pp")

    P.op("sync", lambda e: e.dma_start(out=id_r[:], in_=ident_r), inc=ld, inc_n=16)
    P.op("sync", lambda e: e.dma_start(out=id_f[:], in_=ident_f), inc=ld, inc_n=16)
    consts_ld = ld.v
    P.op("vector", lambda e: e.memset(ones_row[:].bitcast(F32), 1.0))
    P.op("vector", lambda e: e.memset(eps5[:], 1e-5))
    P.op("vector", lambda e: e.memset(eps6[:], 1e-6), inc=vv)
    consts_vv = vv.v

    P1 = ExitStack()
    p1sb = lambda n, s, d: P1.enter_context(nc.sbuf_tensor(n, s, d))
    p1ps = lambda n, s, d: P1.enter_context(nc.psum_tensor(n, s, d))
    rmsw_sb = p1sb("rmsw_sb", [128, DM], F32)
    x_t = [p1sb(f"x_t{i}", [128, DM], F32) for i in range(2)]
    xsq = p1sb("xsq", [128, DM], F32)
    xn_t = p1sb("xn_t", [128, DM], F32R)
    z_t = p1sb("z_t", [128, DM], F32R)
    scol = p1sb("scol", [128, 64], F32)
    s2col = p1sb("s2col", [128, 64], F32)
    sm264 = p1sb("sm264", [128, 64], F32)
    s2n64 = p1sb("s2n64", [128, 64], F32)
    svar64 = p1sb("svar64", [128, 64], F32)
    ln64 = p1sb("ln64", [128, 64], F32)
    rsq64 = p1sb("rsq64", [128, 64], F32)
    rv64 = p1sb("rv64", [128, 64], F32)
    c264 = p1sb("c264", [128, 64], F32)
    ccol = p1sb("ccol", [128, 64], F32)
    vhat64 = p1sb("vhat64", [128, 64], F32)
    rN64 = p1sb("rN64", [128, 64], F32)
    nt64 = p1sb("nt64", [128, 64], F32)
    rpre64 = p1sb("rpre64", [128, 64], F32)
    cpre64 = p1sb("cpre64", [128, 64], F32)
    fm_st = [p1sb(f"fm_st{i}", [128, 512], F32R) for i in range(2)]
    tp_ps = [p1ps(f"tp_ps{i}", [128, 512], F32R) for i in range(2)]

    zdma = P.sem("zdma")
    P.op("sync", lambda e: e.dma_start(out=rmsw_sb[:], in_=rmsw_row), inc=ld, inc_n=16)
    consts_ld = ld.v
    a_done = []
    for tt in range(NTILE):
        buf = x_t[tt % 2]
        w = [(ld, consts_ld)]
        if tt >= 2:
            w.append((vv, a_done[tt - 2]))
        P.op("sync", lambda e, b=buf, tt=tt: e.dma_start(
            out=b[:], in_=x_own[tt * 128:(tt + 1) * 128, :]), waits=w, inc=ld, inc_n=16)
        ld_tile = ld.v
        P.op("vector", lambda e, b=buf: e.tensor_mul(xsq[:], b[:], b[:]),
             waits=[(ld, ld_tile), (vv, consts_vv)])
        P.op("vector", lambda e, b=buf, tt=tt: e.reduce_sum(
            scol[:, tt:tt + 1], b[:], axis=AX))
        P.op("vector", lambda e, tt=tt: e.reduce_sum(
            s2col[:, tt:tt + 1], xsq[:], axis=AX), inc=vv)
        a_done.append(vv.v)
    P.op("gpsimd", lambda e: e.tensor_scalar(
        mcol[:], scol[:], 1.0 / DM, None, ALU.mult), waits=[(vv, a_done[-1])])
    P.op("gpsimd", lambda e: e.tensor_mul(sm264[:], mcol[:], mcol[:]))
    P.op("gpsimd", lambda e: e.tensor_scalar(
        s2n64[:], s2col[:], 1.0 / DM, None, ALU.mult))
    P.op("gpsimd", lambda e: e.tensor_sub(svar64[:], s2n64[:], sm264[:]), inc=pp)
    svar_done = pp.v
    P.op("scalar", lambda e: e.activation(ln64[:], svar64[:], AF.Ln, bias=eps5[:]),
         waits=[(pp, svar_done)])
    P.op("scalar", lambda e: e.activation(rcol[:], ln64[:], AF.Exp, scale=-0.5), inc=sc)
    seed_done = sc.v
    P.op("vector", lambda e: e.tensor_scalar(
        vhat64[:], svar64[:], 1e-5, None, ALU.add), waits=[(sc, seed_done)])
    for step in range(2):
        src = rcol if step == 0 else rN64
        dst = rN64 if step == 0 else rpre64
        P.op("vector", lambda e, src=src: e.tensor_mul(nt64[:], src[:], src[:]))
        P.op("vector", lambda e: e.tensor_mul(nt64[:], nt64[:], vhat64[:]))
        P.op("vector", lambda e: e.tensor_scalar(
            nt64[:], nt64[:], -0.5, 1.5, ALU.mult, ALU.add))
        P.op("vector", lambda e, src=src, dst=dst: e.tensor_mul(
            dst[:], nt64[:], src[:]))
    P.op("vector", lambda e: e.tensor_mul(rsq64[:], rpre64[:], rpre64[:]))
    P.op("vector", lambda e: e.tensor_mul(rv64[:], rsq64[:], svar64[:]))
    P.op("vector", lambda e: e.tensor_scalar(
        vhat64[:], rv64[:], 1e-6, None, ALU.add), inc=vv)
    rv_done = vv.v
    P.op("scalar", lambda e: e.activation(ln64[:], rv64[:], AF.Ln, bias=eps6[:]),
         waits=[(vv, rv_done)])
    P.op("scalar", lambda e: e.activation(c264[:], ln64[:], AF.Exp, scale=-0.5), inc=sc)
    seed2_done = sc.v
    first = True
    for step in range(2):
        src = c264 if step == 0 else rN64
        dst = rN64 if step == 0 else c264
        P.op("vector", lambda e, src=src: e.tensor_mul(nt64[:], src[:], src[:]),
             waits=([(sc, seed2_done)] if first else ()))
        first = False
        P.op("vector", lambda e: e.tensor_mul(nt64[:], nt64[:], vhat64[:]))
        P.op("vector", lambda e: e.tensor_scalar(
            nt64[:], nt64[:], -0.5, 1.5, ALU.mult, ALU.add))
        P.op("vector", lambda e, src=src, dst=dst: e.tensor_mul(
            dst[:], nt64[:], src[:]))
    P.op("vector", lambda e: e.tensor_mul(cpre64[:], c264[:], rpre64[:]), inc=vv)
    nwt_done = vv.v
    P.op("gpsimd", lambda e: e.tensor_copy(rcol[:], rpre64[:]), waits=[(vv, nwt_done)])
    P.op("gpsimd", lambda e: e.tensor_copy(ccol[:], cpre64[:]), inc=pp)
    stats_done = pp.v
    b_done = []
    fm_evac = {}
    n_fm = 0
    for tt in range(NTILE):
        buf = x_t[tt % 2]
        w = [(pp, stats_done)]
        if tt >= 2:
            w.append((vv, b_done[tt - 2]))
        else:
            w.append((vv, a_done[-1]))
        P.op("sync", lambda e, b=buf, tt=tt: e.dma_start(
            out=b[:], in_=x_own[tt * 128:(tt + 1) * 128, :]), waits=w, inc=ld, inc_n=16)
        ld_tile = ld.v
        P.op("vector", lambda e, b=buf, tt=tt: e.tensor_scalar(
            xn_t[:], b[:], mcol[:, tt:tt + 1], rcol[:, tt:tt + 1],
            ALU.subtract, ALU.mult), waits=[(ld, ld_tile), (pp, stats_done)])
        P.op("vector", lambda e, tt=tt: e.scalar_tensor_tensor(
            z_t[:], xn_t[:], ccol[:, tt:tt + 1], rmsw_sb[:], ALU.mult, ALU.mult),
            inc=vv)
        b_done.append(vv.v)
        zx_ready = vv.v
        for half in range(2):
            src = z_t if half == 0 else xn_t
            for grp in range(4):
                pb = tp_ps[n_fm % 2]
                fb = fm_st[n_fm % 2]
                w = [(vv, zx_ready)]
                if n_fm >= 2:
                    w.append((vv, fm_evac[n_fm - 2]))
                for k in range(4):
                    c = grp * 4 + k
                    P.op("tensor", lambda e, pb=pb, src=src, c=c, k=k: e.transpose(
                        pb[:, k * 128:(k + 1) * 128],
                        src[:, c * 128:(c + 1) * 128], id_r[:]),
                        waits=(w if k == 0 else ()), inc=(te if k == 3 else None))
                tp_done = te.v
                w2 = [(te, tp_done)]
                if n_fm >= 2:
                    w2.append((zdma, 16 * (n_fm - 1)))
                P.op("vector", lambda e, pb=pb, fb=fb: e.tensor_copy(fb[:], pb[:]),
                     waits=w2, inc=vv)
                fm_evac[n_fm] = vv.v
                ev_done = vv.v
                dst = z_contrib if half == 0 else xn_dram
                dv = dst.ap().rearrange("(c p) t -> p c t", p=128)
                P.op("sync", lambda e, dv=dv, grp=grp, tt=tt, fb=fb: e.dma_start(
                    out=dv[:, grp * 4:(grp + 1) * 4, tt * 128:(tt + 1) * 128],
                    in_=fb[:].rearrange("p (k t) -> p k t", k=4)),
                    waits=[(vv, ev_done)], inc=zdma, inc_n=16)
                n_fm += 1

    P.op("gpsimd", lambda e: e.collective_compute(
        "AllGather", ALU.bypass, replica_groups=[list(range(8))],
        ins=[z_contrib.ap().opt()], outs=[z_all.ap().opt()]),
        waits=[(zdma, 16 * n_fm)], inc=cc1)
    p1_end = [(zdma, 16 * n_fm), (cc1, 1)]
    if DEBUG_TAPS:
        P.op("sync", lambda e: e.dma_start(out=dbg["d_mc"], in_=mcol[:]),
             waits=[(zdma, 16 * n_fm)], inc=ld, inc_n=16)
        P.op("sync", lambda e: e.dma_start(out=dbg["d_rc"], in_=rcol[:]),
             inc=ld, inc_n=16)
        P.op("sync", lambda e: e.dma_start(out=dbg["d_cc"], in_=ccol[:]),
             inc=ld, inc_n=16)

        P.op("gpsimd", lambda e: e.dma_start(out=dbg["d_z"], in_=z_contrib.ap()),
             waits=[(zdma, 16 * n_fm)], inc=gl, inc_n=16)
        P.op("gpsimd", lambda e: e.dma_start(out=dbg["d_xn"], in_=xn_dram.ap()),
             inc=gl, inc_n=16)
    P1.close()

    A23 = ExitStack()
    Kt = A23.enter_context(nc.sbuf_tensor("Kt", [D1, S], F32R))
    Vt = A23.enter_context(nc.sbuf_tensor("Vt", [128, 64 * D1], F32R))

    KQ = ExitStack()
    ksb = lambda n, s, d: KQ.enter_context(nc.sbuf_tensor(n, s, d))
    kps = lambda n, s, d: KQ.enter_context(nc.psum_tensor(n, s, d))
    z_strip = [ksb(f"z_strip{i}", [128, NCH * STRIP], F32R) for i in range(2)]
    winsb = ksb("winsb", [128, NCH * 384], F32R)
    rq_t = [ksb(f"rq_t{i}", [128, STRIP], F32) for i in range(2)]
    rk_t = [ksb(f"rk_t{i}", [64, STRIP], F32) for i in range(2)]
    stagA = ksb("stagA", [64, STRIP], F32R)
    stagB = ksb("stagB", [64, STRIP], F32R)
    tmpA = ksb("tmpA", [64, STRIP], F32)
    tmpB = ksb("tmpB", [64, STRIP], F32)
    v_stage = ksb("v_stage", [64, STRIP], F32R)
    qstage = ksb("qstage", [64, HPG * STRIP], F32R)
    q_ps = [kps(f"q_ps{i}", [128, STRIP], F32) for i in range(2)]
    kv_ps = kps("kv_ps", [128, STRIP], F32)
    vt_ps = kps("vt_ps", [128, 4 * 64], F32R)

    P.op("vector", lambda e: e.memset(Kt[64:65, :].bitcast(F32), 1.0), waits=p1_end)
    P.op("vector", lambda e: e.memset(
        Vt[:].bitcast(F32).rearrange("p (k d) -> p k d", k=64)[:, :, 64:65].opt(), 1.0),
        inc=vv)
    kvinit = vv.v
    wv = w_in_gT.rearrange("(c p) b -> p c b", p=128)
    P.op("gpsimd", lambda e: e.dma_start(
        out=winsb[:].rearrange("p (c b) -> p c b", b=384)[:, 0:8, :],
        in_=wv[:, 0:8, :]), waits=p1_end, inc=gl, inc_n=16)
    P.op("gpsimd", lambda e: e.dma_start(
        out=winsb[:].rearrange("p (c b) -> p c b", b=384)[:, 8:16, :],
        in_=wv[:, 8:16, :]), inc=gl, inc_n=16)
    win_ld = gl.v

    zav = z_all.ap().rearrange("(r c p) t -> r p c t", r=8, p=128)
    rope_done = []
    qdma = P.sem("qdma")
    for s in range(NSTRIP):
        r, ss = s // 2, s % 2
        zb = z_strip[s % 2]
        w = list(p1_end)
        if s >= 2:
            w.append((vv, rope_done[s - 2]))
        for piece in range(2):
            P.op("sync", lambda e, zb=zb, r=r, ss=ss, piece=piece: e.dma_start(
                out=zb[:].rearrange("p (c t) -> p c t", c=NCH)[:, piece * 8:(piece + 1) * 8, :],
                in_=zav[r, :, piece * 8:(piece + 1) * 8, ss * 512:(ss + 1) * 512]),
                waits=(w if piece == 0 else ()), inc=ld, inc_n=16)
        P.op("sync", lambda e, s=s, b=rq_t[s % 2]: e.dma_start(
            out=b[:], in_=ropeq[:, s * 512:(s + 1) * 512]), inc=ld, inc_n=16)
        P.op("sync", lambda e, s=s, b=rk_t[s % 2]: e.dma_start(
            out=b[:], in_=ropek[:, s * 512:(s + 1) * 512]), inc=ld, inc_n=16)
        z_ld = ld.v
        w = [(ld, z_ld), (gl, win_ld)]
        if s >= 1:
            w.append((vv, rope_done[s - 1]))
        for c in range(NCH):
            for b in range(3):
                ps = q_ps[b] if b < 2 else kv_ps
                P.op("tensor", lambda e, ps=ps, c=c, b=b, zb=zb: e.matmul(
                    ps[:], winsb[:, c * 384 + b * 128:c * 384 + (b + 1) * 128],
                    zb[:, c * 512:(c + 1) * 512],
                    start=(c == 0), stop=(c == NCH - 1)),
                    waits=(w if (c == 0 and b == 0) else ()),
                    inc=(te if (c == NCH - 1 and b == 2) else None))
        mm_done = te.v
        for b in range(2):
            qp = q_ps[b]
            rqc = rq_t[s % 2]
            wq = [(te, mm_done), (vv, kvinit), (pp, pp.v)]
            P.op("vector", lambda e, qp=qp, rqc=rqc: e.tensor_mul(
                tmpA[:], qp[0:64, :], rqc[0:64, :]), waits=wq)
            P.op("vector", lambda e, qp=qp, rqc=rqc: e.tensor_mul(
                tmpB[:], qp[64:128, :], rqc[64:128, :]))
            P.op("vector", lambda e: e.tensor_sub(stagA[:], tmpA[:], tmpB[:]))
            P.op("vector", lambda e, qp=qp, rqc=rqc: e.tensor_mul(
                tmpA[:], qp[64:128, :], rqc[0:64, :]))
            P.op("vector", lambda e, qp=qp, rqc=rqc: e.tensor_mul(
                tmpB[:], qp[0:64, :], rqc[64:128, :]))
            P.op("vector", lambda e: e.tensor_add(stagB[:], tmpA[:], tmpB[:]), inc=vv)
            st_ready = vv.v
            for idx, (srcT, srclo, hd, dstlo) in enumerate((
                    (stagA, 0, 2 * b, 0), (stagA, 32, 2 * b + 1, 0),
                    (stagB, 0, 2 * b, 32), (stagB, 32, 2 * b + 1, 32))):
                P.op("gpsimd", lambda e, srcT=srcT, srclo=srclo, hd=hd, dstlo=dstlo:
                     e.tensor_copy(
                         qstage[dstlo:dstlo + 32, hd * 512:(hd + 1) * 512],
                         srcT[srclo:srclo + 32, :]),
                     waits=[(vv, st_ready)], inc=(pp if idx == 3 else None))
        pool_done = pp.v
        rkc = rk_t[s % 2]
        P.op("vector", lambda e, rkc=rkc: e.tensor_mul(
            tmpA[0:32, :], kv_ps[0:32, :], rkc[0:32, :]), waits=[(te, mm_done)])
        P.op("vector", lambda e, rkc=rkc: e.tensor_mul(
            tmpB[0:32, :], kv_ps[32:64, :], rkc[32:64, :]))
        P.op("vector", lambda e, s=s: e.tensor_sub(
            Kt[0:32, s * 512:(s + 1) * 512], tmpA[0:32, :], tmpB[0:32, :]))
        P.op("vector", lambda e, rkc=rkc: e.tensor_mul(
            tmpA[0:32, :], kv_ps[32:64, :], rkc[0:32, :]))
        P.op("vector", lambda e, rkc=rkc: e.tensor_mul(
            tmpB[0:32, :], kv_ps[0:32, :], rkc[32:64, :]))
        P.op("vector", lambda e, s=s: e.tensor_add(
            Kt[32:64, s * 512:(s + 1) * 512], tmpA[0:32, :], tmpB[0:32, :]))
        P.op("vector", lambda e: e.tensor_copy(v_stage[:], kv_ps[64:128, :]), inc=vv)
        vst_ready = vv.v
        for k in range(4):
            P.op("tensor", lambda e, k=k: e.transpose(
                vt_ps[:, k * 64:(k + 1) * 64], v_stage[:, k * 128:(k + 1) * 128],
                id_r[0:64, 0:64]), waits=([(vv, vst_ready)] if k == 0 else ()),
                inc=(te if k == 3 else None))
        vtp_done = te.v
        for k in range(4):
            kt_idx = 4 * s + k
            P.op("vector", lambda e, k=k, kt_idx=kt_idx: e.tensor_copy(
                Vt[:, kt_idx * D1:kt_idx * D1 + 64], vt_ps[:, k * 64:(k + 1) * 64]),
                waits=[(te, vtp_done)], inc=(vv if k == 3 else None))
        rope_done.append(vv.v)
        qdv = q_dram.ap().rearrange("(j p) t -> p j t", p=D1)
        P.op("sync", lambda e, s=s, qdv=qdv: e.dma_start(
            out=qdv[0:64, :, s * 512:(s + 1) * 512],
            in_=qstage[:].rearrange("p (j t) -> p j t", j=4)),
            waits=[(pp, pool_done)], inc=qdma, inc_n=16)

    kvq_end = [(qdma, 16 * NSTRIP), (vv, rope_done[-1])]
    if DEBUG_TAPS:
        P.op("gpsimd", lambda e: e.dma_start(out=dbg["d_q"], in_=q_dram.ap()),
             waits=kvq_end, inc=gl, inc_n=16)
        P.op("sync", lambda e: e.dma_start(out=dbg["d_K"], in_=Kt[:]),
             waits=kvq_end, inc=ld, inc_n=16)
        P.op("sync", lambda e: e.dma_start(out=dbg["d_V"], in_=Vt[:]),
             waits=kvq_end, inc=ld, inc_n=16)
    KQ.close()

    A3 = ExitStack()
    asb = lambda n, s, d: A3.enter_context(nc.sbuf_tensor(n, s, d))
    aps = lambda n, s, d: A3.enter_context(nc.psum_tensor(n, s, d))
    o_loc = A23.enter_context(nc.sbuf_tensor("o_loc", [128, 2 * S], F32R))
    qh_t = asb("qh", [D1, S], F32R)
    qh = [qh_t, qh_t]
    qabs_t = asb("qabs_t", [64, STRIP], F32R)
    kmax_f = asb("kmax_f", [64, 1], F32)
    kmax_r = asb("kmax_r", [64, 1], F32R)
    Pt = [asb(f"Pt{i}", [128, STRIP], F32R) for i in range(3)]
    mask_sb = asb("mask_sb", [128, 4 * STRIP], F32R)
    rl_f = asb("rl_f", [1, STRIP], F32)
    rl_f2 = asb("rl_f2", [1, STRIP], F32)
    rl_t = asb("rl_t", [1, STRIP], F32)
    rl_sb = asb("rl_sb", [1, STRIP], F32R)
    o_stage = asb("o_stage", [64, STRIP], F32R)
    S_ps = [aps(f"S_ps{i}", [128, STRIP], F32) for i in range(2)]
    o_ps = [aps(f"o_ps{i}", [D1, STRIP], F32) for i in range(2)]
    bnd_ps = [aps(f"bnd_ps{i}", [1, STRIP], F32) for i in range(2)]
    rb_ps = aps("rb_ps", [64, STRIP], F32)

    P.op("sync", lambda e: e.dma_start(out=mask_sb[:], in_=dmasks),
         waits=kvq_end, inc=ld, inc_n=16)
    mask_ld = ld.v
    P.op("vector", lambda e: e.reduce_max(
        kmax_f[:], Kt[0:64, :], axis=AX, apply_absolute_value=True),
        waits=kvq_end, inc=vv)
    P.op("gpsimd", lambda e: e.tensor_copy(kmax_r[:], kmax_f[:]),
         waits=[(vv, vv.v)], inc=pp)
    kmax_sem, kmax_ready = pp, pp.v

    sqk = P.sem("sqk")
    sxp = P.sem("sxp")
    smk = P.sem("smk")
    sav = P.sem("sav")
    slr = P.sem("slr")
    srb = P.sem("srb")
    son = P.sem("son")
    sqd = P.sem("sqd")
    smr = P.sem("smr")

    qdv2 = q_dram.ap().rearrange("(j p) t -> j p t", p=D1)
    it = 0
    nmask = 0
    pend_av = []
    j_end_av = {}

    def emit_av(rec):
        (kt, i, masked, nm, jj, is_first, is_last, obank) = rec
        w = [(smk, nm)] if masked else [(sxp, i + 1)]
        if is_first and jj >= 2:
            w.append((son, jj - 1))
        P.op("tensor", lambda e, kt=kt, i=i, is_first=is_first, is_last=is_last,
             obank=obank: e.matmul(
                 o_ps[obank][:], Vt[:, kt * D1:(kt + 1) * D1], Pt[i % 3][:],
                 start=is_first, stop=is_last), waits=w, inc=sav)

    for h in range(HPG):
        w = [(qdma, 16 * NSTRIP)]
        if h >= 1:
            w.append((sav, j_end_av[(h - 1) * 16 + 15]))
            w.append((son, h * 16))
        P.op("sync", lambda e, h=h: e.dma_start(
            out=qh[h % 2][0:64, :], in_=qdv2[h, 0:64, :]), waits=w, inc=sqd, inc_n=16)
        q_ld = sqd.v
        for qs in range(NSTRIP):
            P.op("scalar", lambda e, h=h, qs=qs: e.activation(
                qabs_t[:], qh[h % 2][0:64, qs * 512:(qs + 1) * 512], AF.Abs),
                waits=[(sqd, q_ld), (te, te.v)], inc=sc)
            qa_ready = sc.v
            bb = bnd_ps[qs % 2]
            wb = [(sc, qa_ready), (kmax_sem, kmax_ready)]
            if h * 16 + qs >= 2:
                wb.append((smr, h * 16 + qs - 1))
            P.op("tensor", lambda e, bb=bb: e.matmul(
                bb[:], kmax_r[:], qabs_t[:], start=True, stop=True),
                waits=wb, inc=te)
            bmm = te.v
            P.op("vector", lambda e, h=h, qs=qs, bb=bb: e.tensor_scalar(
                qh[h % 2][64:65, qs * 512:(qs + 1) * 512], bb[0:1, :],
                -1.0, -1.0, ALU.mult, ALU.add), waits=[(te, bmm)], inc=smr)
        mrow_base = smr.v - NSTRIP
        for qs in range(NSTRIP):
            jj = h * 16 + qs
            nkt = 4 * qs + 4
            obank = jj % 2
            for kt in range(nkt):
                i = it
                masked = kt >= 4 * qs
                w = [(smr, mrow_base + qs + 1)]
                if i >= 2:
                    w.append((sxp, i - 1))
                P.op("tensor", lambda e, kt=kt, i=i, h=h, qs=qs: e.matmul(
                    S_ps[i % 2][:], Kt[:, kt * 128:(kt + 1) * 128],
                    qh[h % 2][:, qs * 512:(qs + 1) * 512],
                    start=True, stop=True), waits=w, inc=sqk)
                w = [(sqk, i + 1), (ld, mask_ld)]
                if i >= 3:
                    w.append((sav, i - 2))
                P.op("scalar", lambda e, i=i: e.activation(
                    Pt[i % 3][:], S_ps[i % 2][:], AF.Exp), waits=w, inc=sxp)
                if masked:
                    d = kt - 4 * qs
                    nmask += 1
                    P.op("vector", lambda e, i=i, d=d: e.tensor_mul(
                        Pt[i % 3][:], Pt[i % 3][:],
                        mask_sb[:, d * 512:(d + 1) * 512]),
                        waits=[(sxp, i + 1)], inc=smk)
                pend_av.append((kt, i, masked, nmask, jj, kt == 0, kt == nkt - 1, obank))
                if len(pend_av) >= 2:
                    emit_av(pend_av.pop(0))
                it += 1
            while pend_av:
                emit_av(pend_av.pop(0))
            j_end_av[jj] = sav.v
            P.op("scalar", lambda e, obank=obank: e.activation(
                rl_f[:], o_ps[obank][64:65, :], AF.Ln), waits=[(sav, sav.v)])
            P.op("scalar", lambda e: e.activation(
                rl_f2[:], rl_f[:], AF.Exp, scale=-1.0), inc=sc)
            rl_seed = sc.v
            P.op("vector", lambda e, obank=obank: e.tensor_mul(
                rl_t[:], rl_f2[:], o_ps[obank][64:65, :]), waits=[(sc, rl_seed)])
            P.op("vector", lambda e: e.tensor_scalar(
                rl_t[:], rl_t[:], -1.0, 2.0, ALU.mult, ALU.add))
            P.op("vector", lambda e: e.tensor_mul(
                rl_sb[:], rl_t[:], rl_f2[:]), inc=slr)
            lr_done = slr.v
            w = [(slr, lr_done)]
            if jj >= 1:
                w.append((son, jj))
            P.op("tensor", lambda e: e.matmul(
                rb_ps[:], ones_row[0:1, 0:64], rl_sb[:], start=True, stop=True),
                waits=w, inc=srb)
            rb_done = srb.v
            P.op("vector", lambda e, obank=obank: e.tensor_copy(
                o_stage[:], o_ps[obank][0:64, :]), waits=[(srb, rb_done)])
            c2 = h // 2
            plo = (h % 2) * 64
            P.op("vector", lambda e, c2=c2, plo=plo, qs=qs: e.tensor_mul(
                o_loc[plo:plo + 64, c2 * S + qs * 512:c2 * S + (qs + 1) * 512],
                o_stage[:], rb_ps[:]), inc=son)

    attn_done = [(son, son.v)]
    if DEBUG_TAPS:
        P.op("sync", lambda e: e.dma_start(out=dbg["d_oloc"], in_=o_loc[:]),
             waits=attn_done, inc=ld, inc_n=16)
    A3.close()

    WO = ExitStack()
    wsb = lambda n, s, d: WO.enter_context(nc.sbuf_tensor(n, s, d))
    wps = lambda n, s, d: WO.enter_context(nc.psum_tensor(n, s, d))
    wog_sb = wsb("wog_sb", [128, 2 * DM], F32R)
    y_stage = [wsb(f"y_stage{i}", [128, STRIP], F32R) for i in range(4)]
    yw_ps = [wps(f"yw_ps{i}", [128, STRIP], F32) for i in range(2)]

    P.op("gpsimd", lambda e: e.dma_start(
        out=wog_sb[:].rearrange("p (c n) -> p c n", c=2),
        in_=w_og.rearrange("(c p) n -> p c n", p=128)), waits=attn_done,
        inc=gl, inc_n=16)
    wog_ld = gl.v
    syd = P.sem("syd")
    syv = P.sem("syv")
    n_yd = 0
    for tt64 in range(64):
        for ds in range(4):
            gidx = tt64 * 4 + ds
            bank = yw_ps[gidx % 2]
            w = [(gl, wog_ld)] + attn_done
            if gidx >= 2:
                w.append((syv, gidx - 1))
            for c2 in range(2):
                P.op("tensor", lambda e, bank=bank, c2=c2, tt64=tt64, ds=ds: e.matmul(
                    bank[:], o_loc[:, c2 * S + tt64 * 128:c2 * S + (tt64 + 1) * 128],
                    wog_sb[:, c2 * DM + ds * 512:c2 * DM + (ds + 1) * 512],
                    start=(c2 == 0), stop=(c2 == 1)),
                    waits=(w if c2 == 0 else ()), inc=(te if c2 == 1 else None))
            ymm = te.v
            w2 = [(te, ymm)]
            if gidx >= 4:
                w2.append((syd, 16 * (gidx - 3)))
            P.op("vector", lambda e, bank=bank, gidx=gidx: e.tensor_copy(
                y_stage[gidx % 4][:], bank[:]), waits=w2, inc=syv)
            yv = syv.v
            P.op("sync", lambda e, tt64=tt64, ds=ds, gidx=gidx: e.dma_start(
                out=y_part.ap()[tt64 * 128:(tt64 + 1) * 128, ds * 512:(ds + 1) * 512],
                in_=y_stage[gidx % 4][:]), waits=[(syv, yv)], inc=syd, inc_n=16)
            n_yd += 1

    P.op("gpsimd", lambda e: e.collective_compute(
        "ReduceScatter", ALU.add, replica_groups=[list(range(8))],
        ins=[y_part.ap().opt()], outs=[y_own.ap().opt()]),
        waits=[(syd, 16 * n_yd)], inc=cc2)
    wo_end = [(syd, 16 * n_yd), (syv, syv.v)]
    if DEBUG_TAPS:
        P.op("gpsimd", lambda e: e.dma_start(out=dbg["d_yp"], in_=y_part.ap()),
             waits=wo_end, inc=gl, inc_n=16)
        P.op("gpsimd", lambda e: e.dma_start(out=dbg["d_yo"], in_=y_own.ap()),
             waits=[(cc2, 1)], inc=gl, inc_n=16)
    WO.close()
    A23.close()

    F2 = ExitStack()
    fsb = lambda n, s, d: F2.enter_context(nc.sbuf_tensor(n, s, d))
    fps = lambda n, s, d: F2.enter_context(nc.psum_tensor(n, s, d))
    xn_s = fsb("xn_s", [128, NCH * STRIP], F32R)
    wslot = [fsb(f"wslot{i}", [128, 8 * 1024], F32R) for i in range(2)]
    act_sb = fsb("act_sb", [128, 16 * 512], F32R)
    routed = fsb("routed", [128, 4 * DM], F32)
    gw_sb = fsb("gw_sb", [128, NCH * NE], F32R)
    eb_sb = fsb("eb_sb", [128, NE], F32)
    sel_sb = fsb("sel_sb", [NE, 128], F32R)
    lg_sb = fsb("lg_sb", [16, STRIP], F32)
    ltok = fsb("ltok", [128, 64], F32)
    th_t = fsb("th_t", [128, 64], F32)
    scrs = fsb("scrs", [128, 64], F32)
    s4c = fsb("s4c", [128, 64], F32)
    pair_t = fsb("pair_t", [128, 24], F32)
    mx_t = fsb("mx_t", [128, 20], F32)
    mxl = fsb("mxl", [128, 16], F32)
    gs_l = fsb("gs_l", [128, 4], F32)
    tp1 = fsb("tp1", [128, 16], F32)
    tp2 = fsb("tp2", [128, 16], F32)
    gselt = fsb("gselt", [128, 4], F32)
    pen_t = fsb("pen_t", [128, 4], F32)
    sec_t = fsb("sec_t", [128, 1], F32)
    mskd = fsb("mskd", [128, 16], F32)
    mxt = fsb("mxt", [128, 1], F32)
    Wt = fsb("Wt", [128, 64], F32)
    rs_t = fsb("rs_t", [128, 1], F32)
    Wfm = fsb("Wfm", [16, STRIP], F32R)
    tsil = [fsb(f"tsil{i}", [128, STRIP], F32) for i in range(2)]
    v1t = [fsb(f"v1t{i}", [128, STRIP], F32) for i in range(2)]
    gstage = fsb("gstage", [128, DM], F32)
    gu_ps = [fps(f"gu_ps{i}", [128, STRIP], F32) for i in range(4)]
    wb_ps = fps("wb_ps", [128, STRIP], F32)
    y_ps = [fps(f"y_ps{i}", [128, STRIP], F32) for i in range(2)]
    lg_ps = fps("lg_ps", [16, STRIP], F32)

    P.op("gpsimd", lambda e: e.dma_start(
        out=gw_sb[:].rearrange("p (c n) -> p c n", c=NCH),
        in_=gwTd.rearrange("(c p) n -> p c n", p=128)), waits=wo_end, inc=gl, inc_n=16)
    P.op("sync", lambda e: e.dma_start(out=eb_sb[:], in_=ebias), waits=wo_end,
         inc=ld, inc_n=16)
    f2_static = [(gl, gl.v), (ld, ld.v)]
    ssel = P.sem("ssel")

    xnv = xn_dram.ap().rearrange("(c p) t -> p c t", p=128)
    swl = P.sem("swl")
    swf = P.sem("swf")
    sas = P.sem("sas")
    sra = P.sem("sra")
    sgd = P.sem("sgd")
    svh = P.sem("svh")
    wctr = [0]

    def wload(in3, waits=()):
        slot = wslot[wctr[0] % 2]
        A = in3.shape[1]
        B = in3.shape[2]
        w = list(waits)
        if wctr[0] >= 2:
            w.append((swf, wctr[0] - 1))
        assert A * B == 8 * 1024
        P.op("gpsimd", lambda e, slot=slot, in3=in3, A=A: e.dma_start(
            out=slot[:].rearrange("p (a b) -> p a b", a=A),
            in_=in3), waits=w, inc=swl, inc_n=16)
        wctr[0] += 1
        return slot, swl.v

    strip_done = []
    for ss in range(2):
        w = list(f2_static)
        if ss > 0:
            w.append((sgd, strip_done[-1]))
        for piece in range(2):
            P.op("sync", lambda e, ss=ss, piece=piece: e.dma_start(
                out=xn_s[:].rearrange("p (c t) -> p c t", c=NCH)[:, piece * 8:(piece + 1) * 8, :],
                in_=xnv[:, piece * 8:(piece + 1) * 8, ss * 512:(ss + 1) * 512]),
                waits=(w if piece == 0 else ()), inc=ld, inc_n=16)
        xn_ld = ld.v
        for c in range(NCH):
            P.op("tensor", lambda e, c=c: e.matmul(
                lg_ps[:], gw_sb[:, c * NE:(c + 1) * NE],
                xn_s[:, c * 512:(c + 1) * 512],
                start=(c == 0), stop=(c == NCH - 1)),
                waits=([(ld, xn_ld)] + ([(vv, vv.v), (pp, pp.v)] if ss > 0 else []))
                if c == 0 else (), inc=(te if c == NCH - 1 else None))
        lgmm = te.v
        P.op("vector", lambda e: e.tensor_copy(lg_sb[:], lg_ps[:]),
             waits=[(te, lgmm)], inc=vv)
        lgev = vv.v
        for k in range(4):
            P.op("tensor", lambda e, k=k: e.transpose(
                y_ps[0][:, k * 16:(k + 1) * 16],
                lg_sb[0:16, k * 128:(k + 1) * 128], id_f[0:16, 0:16]),
                waits=([(vv, lgev)] if k == 0 else ()),
                inc=(te if k == 3 else None))
        lgtp = te.v
        P.op("vector", lambda e: e.tensor_copy(ltok[:], y_ps[0][:, 0:64]),
             waits=[(te, lgtp)], inc=vv)
        ltok_done = vv.v
        P.op("scalar", lambda e: e.activation(th_t[:], ltok[:], AF.Tanh, scale=0.5),
             waits=[(vv, ltok_done)], inc=sc)
        th_done = sc.v
        P.op("vector", lambda e: e.tensor_scalar(
            scrs[:], th_t[:], 0.5, 0.5, ALU.mult, ALU.add),
            waits=[(sc, th_done), (ld, f2_static[1][1])], inc=vv)
        scrs_done = vv.v
        for k in range(4):
            P.op("gpsimd", lambda e, k=k: e.tensor_add(
                s4c[:, k * 16:(k + 1) * 16], scrs[:, k * 16:(k + 1) * 16], eb_sb[:]),
                waits=([(vv, scrs_done)] if k == 0 else ()))
        for k in range(4):
            b0 = k * 16
            vw = lambda i, b0=b0: s4c[:, b0:b0 + 16].rearrange(
                "p (g e) -> p g e", g=4)[:, :, i:i + 1].opt()
            pairs = [(0, 1), (2, 3), (0, 2), (1, 3), (0, 3), (1, 2)]
            for pi, (a, bq) in enumerate(pairs):
                P.op("gpsimd", lambda e, a=a, bq=bq, pi=pi, vw=vw: e.tensor_add(
                    pair_t[:, pi * 4:(pi + 1) * 4], vw(a), vw(bq)),
                    inc=(pp if pi == 5 else None))
            P.op("vector", lambda e: e.tensor_max(
                mx_t[:, 0:4], pair_t[:, 0:4], pair_t[:, 4:8]), waits=[(pp, pp.v)])
            P.op("vector", lambda e: e.tensor_max(
                mx_t[:, 4:8], pair_t[:, 8:12], pair_t[:, 12:16]))
            P.op("vector", lambda e: e.tensor_max(
                mx_t[:, 8:12], pair_t[:, 16:20], pair_t[:, 20:24]), inc=vv)
            P.op("gpsimd", lambda e: e.tensor_copy(mxl[:, 0:12], mx_t[:, 0:12]),
                 waits=[(vv, vv.v)], inc=pp)
            P.op("vector", lambda e: e.tensor_max(
                mx_t[:, 12:16], mxl[:, 0:4], mxl[:, 4:8]), waits=[(pp, pp.v)], inc=vv)
            P.op("gpsimd", lambda e: e.tensor_copy(mxl[:, 12:16], mx_t[:, 12:16]),
                 waits=[(vv, vv.v)], inc=pp)
            P.op("vector", lambda e: e.tensor_max(
                mx_t[:, 16:20], mxl[:, 12:16], mxl[:, 8:12]), waits=[(pp, pp.v)], inc=vv)
            P.op("gpsimd", lambda e: e.tensor_copy(gs_l[:], mx_t[:, 16:20]),
                 waits=[(vv, vv.v)], inc=pp)
            P.op("vector", lambda e: e.tensor_max(
                mx_t[:, 0:1], gs_l[:, 0:1], gs_l[:, 1:2]), waits=[(pp, pp.v)])
            P.op("vector", lambda e: e.tensor_max(
                mx_t[:, 1:2], gs_l[:, 2:3], gs_l[:, 3:4]), inc=vv)
            P.op("gpsimd", lambda e: e.tensor_copy(mxl[:, 0:2], mx_t[:, 0:2]),
                 waits=[(vv, vv.v)], inc=pp)
            P.op("vector", lambda e: e.tensor_max(
                mx_t[:, 2:3], gs_l[:, 1:2], mxl[:, 1:2]), waits=[(pp, pp.v)])
            P.op("vector", lambda e: e.tensor_max(
                mx_t[:, 3:4], gs_l[:, 0:1], mxl[:, 1:2]))
            P.op("vector", lambda e: e.tensor_max(
                mx_t[:, 4:5], gs_l[:, 3:4], mxl[:, 0:1]))
            P.op("vector", lambda e: e.tensor_max(
                mx_t[:, 5:6], gs_l[:, 2:3], mxl[:, 0:1]), inc=vv)
            P.op("gpsimd", lambda e: e.tensor_copy(mxl[:, 2:6], mx_t[:, 2:6]),
                 waits=[(vv, vv.v)], inc=pp)
            P.op("vector", lambda e: e.tensor_tensor(
                mx_t[:, 6:7], mxl[:, 2:3], mxl[:, 3:4], ALU.min), waits=[(pp, pp.v)])
            P.op("vector", lambda e: e.tensor_tensor(
                mx_t[:, 7:8], mxl[:, 4:5], mxl[:, 5:6], ALU.min), inc=vv)
            P.op("gpsimd", lambda e: e.tensor_copy(mxl[:, 6:8], mx_t[:, 6:8]),
                 waits=[(vv, vv.v)], inc=pp)
            P.op("vector", lambda e: e.tensor_tensor(
                mx_t[:, 8:9], mxl[:, 6:7], mxl[:, 7:8], ALU.min),
                waits=[(pp, pp.v)], inc=vv)
            P.op("gpsimd", lambda e: e.tensor_copy(sec_t[:], mx_t[:, 8:9]),
                 waits=[(vv, vv.v)])
            P.op("gpsimd", lambda e: e.tensor_single_scalar(
                gselt[:], gs_l[:], sec_t[:], ALU.is_ge))
            P.op("gpsimd", lambda e: e.tensor_scalar(
                pen_t[:], gselt[:], 1e30, -1e30, ALU.mult, ALU.add))
            for j in range(4):
                P.op("gpsimd", lambda e, j=j, b0=b0: e.tensor_single_scalar(
                    mskd[:, j * 4:(j + 1) * 4], s4c[:, b0 + j * 4:b0 + (j + 1) * 4],
                    pen_t[:, j:j + 1], ALU.add),
                    inc=(pp if j == 3 else None))
            for itk in range(4):
                P.op("vector", lambda e: e.reduce_max(mxt[:], mskd[:], axis=AX),
                     waits=[(pp, pp.v)], inc=vv)
                P.op("gpsimd", lambda e: e.tensor_single_scalar(
                    tp1[:, 0:16], mskd[:], mxt[:], ALU.is_ge),
                    waits=[(vv, vv.v)], inc=pp)
                P.op("vector", lambda e, b0=b0: e.tensor_mul(
                    tp2[:, 0:16], scrs[:, b0:b0 + 16], tp1[:, 0:16]),
                    waits=[(pp, pp.v)], inc=vv)
                if itk == 0:
                    P.op("gpsimd", lambda e, b0=b0: e.tensor_copy(
                        Wt[:, b0:b0 + 16], tp2[:, 0:16]), waits=[(vv, vv.v)])
                else:
                    P.op("gpsimd", lambda e, b0=b0: e.tensor_add(
                        Wt[:, b0:b0 + 16], Wt[:, b0:b0 + 16], tp2[:, 0:16]),
                        waits=[(vv, vv.v)])
                P.op("gpsimd", lambda e: e.tensor_scalar(
                    gselt[:, 0:4].bitcast(F32) if False else tp2[:, 0:16],
                    tp1[:, 0:16], -2e30, None, ALU.mult))
                P.op("gpsimd", lambda e: e.tensor_add(
                    mskd[:], mskd[:], tp2[:, 0:16]), inc=pp)
            P.op("vector", lambda e, b0=b0: e.reduce_sum(
                rs_t[:], Wt[:, b0:b0 + 16], axis=AX), waits=[(pp, pp.v)], inc=vv)
            P.op("gpsimd", lambda e: e.tensor_single_scalar(
                sec_t[:], rs_t[:], 1e-20, ALU.add), waits=[(vv, vv.v)], inc=pp)
            P.op("vector", lambda e: e.reciprocal(rs_t[:], sec_t[:]),
                 waits=[(pp, pp.v)], inc=vv)
            P.op("gpsimd", lambda e, b0=b0: e.tensor_scalar(
                Wt[:, b0:b0 + 16], Wt[:, b0:b0 + 16], rs_t[:], None, ALU.mult),
                waits=[(vv, vv.v)], inc=pp)
        W_done = pp.v
        for k in range(4):
            P.op("tensor", lambda e, k=k: e.transpose(
                wb_ps[0:16, k * 128:(k + 1) * 128],
                Wt[:, k * 16:(k + 1) * 16], id_f[:]),
                waits=([(pp, W_done)] if k == 0 else ()), inc=(te if k == 3 else None))
        wtp = te.v
        P.op("vector", lambda e: e.tensor_scalar(
            Wfm[:], wb_ps[0:16, :], 1.25, None, ALU.mult),
            waits=[(te, wtp)], inc=vv)
        wfm_done = vv.v

        def gu_act(nhalf, gview, uview, use_wb, e_idx):
            for hh in range(nhalf):
                gs_, glc = wload(gview(hh), waits=[(ld, xn_ld)])
                us_, ulc = wload(uview(hh))
                for fc in range(4):
                    fcg = hh * 4 + fc
                    gb = gu_ps[(fcg % 2) * 2]
                    ub = gu_ps[(fcg % 2) * 2 + 1]
                    w = [(swl, ulc)]
                    if use_wb:
                        w.append((te, wb_mm))
                    if fcg >= 2:
                        w.append((sas, sas.v - 1))
                    wtsil = [(sas, sas.v - 1)] if fcg >= 2 else []
                    for c in range(NCH):
                        P.op("tensor", lambda e, gb=gb, gs_=gs_, c=c, fc=fc: e.matmul(
                            gb[:], gs_[:, c * 512 + fc * 128:c * 512 + (fc + 1) * 128],
                            xn_s[:, c * 512:(c + 1) * 512],
                            start=(c == 0), stop=(c == NCH - 1)),
                            waits=(w if c == 0 else ()),
                            inc=(swf if (c == NCH - 1 and fc == 3) else None))
                    for c in range(NCH):
                        P.op("tensor", lambda e, ub=ub, us_=us_, c=c, fc=fc: e.matmul(
                            ub[:], us_[:, c * 512 + fc * 128:c * 512 + (fc + 1) * 128],
                            xn_s[:, c * 512:(c + 1) * 512],
                            start=(c == 0), stop=(c == NCH - 1)),
                            inc=(swf if (c == NCH - 1 and fc == 3) else
                                 (te if c == NCH - 1 else None)))
                    gu_done = te.v if not (fc == 3) else te.v
                    wch = [(swf, swf.v)] if fc == 3 else [(te, te.v)]
                    P.op("scalar", lambda e, gb=gb, fcg=fcg: e.activation(
                        tsil[fcg % 2][:], gb[:], AF.Tanh, scale=0.5),
                        waits=wch + wtsil, inc=sc)
                    ts_done = sc.v
                    P.op("vector", lambda e, ub=ub, fcg=fcg: e.scalar_tensor_tensor(
                        v1t[fcg % 2][:], tsil[fcg % 2][:], 1.0, ub[:],
                        ALU.add, ALU.mult), waits=[(sc, ts_done)])
                    P.op("vector", lambda e, gb=gb, fcg=fcg: e.tensor_mul(
                        v1t[fcg % 2][:], v1t[fcg % 2][:], gb[:]))
                    if use_wb:
                        P.op("vector", lambda e, fcg=fcg: e.tensor_mul(
                            act_sb[:, fcg * 512:(fcg + 1) * 512],
                            v1t[fcg % 2][:], wb_ps[:, 0:512]), inc=sas)
                    else:
                        P.op("vector", lambda e, fcg=fcg: e.tensor_scalar(
                            act_sb[:, fcg * 512:(fcg + 1) * 512],
                            v1t[fcg % 2][:], 0.5, None, ALU.mult), inc=sas)
            return sas.v

        def down_phase(nfc, dview, npiece, piece_cols, first_write):
            for pc in range(npiece):
                ds_, dlc = wload(dview(pc))
                nds = piece_cols // 512
                for tt in range(4):
                    for dsl in range(nds):
                        gidx = (pc * 4 + tt) * nds + dsl
                        bank = y_ps[(tt * nds + dsl) % 2]
                        w = [(swl, dlc), (sas, act_target)]
                        if True:
                            w.append((sra, max(0, sra.v - 1)))
                        for c in range(nfc):
                            P.op("tensor", lambda e, bank=bank, ds_=ds_, c=c, tt=tt,
                                 dsl=dsl, piece_cols=piece_cols: e.matmul(
                                     bank[:],
                                     act_sb[:, c * 512 + tt * 128:c * 512 + (tt + 1) * 128],
                                     ds_[:, c * piece_cols + dsl * 512:
                                         c * piece_cols + (dsl + 1) * 512],
                                     start=(c == 0), stop=(c == nfc - 1)),
                                 waits=(w if c == 0 else ()),
                                 inc=(swf if (c == nfc - 1 and tt == 3 and dsl == nds - 1)
                                      else (te if c == nfc - 1 else None)))
                        dmm = swf.v if (tt == 3 and dsl == nds - 1) else te.v
                        col0 = tt * DM + pc * piece_cols + dsl * 512
                        wsem = (swf, dmm) if (tt == 3 and dsl == nds - 1) else (te, dmm)
                        if first_write:
                            P.op("vector", lambda e, bank=bank, col0=col0: e.tensor_copy(
                                routed[:, col0:col0 + 512], bank[:]),
                                waits=[wsem], inc=sra)
                        else:
                            P.op("vector", lambda e, bank=bank, col0=col0: e.tensor_add(
                                routed[:, col0:col0 + 512], routed[:, col0:col0 + 512],
                                bank[:]), waits=[wsem], inc=sra)

        for e_idx in range(NE):
            P.op("sync", lambda e, e_idx=e_idx: e.dma_start(
                out=sel_sb[:], in_=selc[:, e_idx * 128:(e_idx + 1) * 128]),
                waits=[(te, te.v)], inc=ssel, inc_n=16)
            sel_ld = ssel.v
            P.op("tensor", lambda e: e.matmul(
                wb_ps[:, 0:512], sel_sb[:], Wfm[:], start=True, stop=True),
                waits=[(vv, wfm_done), (sas, sas.v), (ssel, sel_ld)], inc=te)
            wb_mm = te.v
            gv = lambda hh, e_idx=e_idx: gateT[e_idx].rearrange(
                "(c p) f -> p c f", p=128)[:, :, hh * 512:(hh + 1) * 512]
            uv = lambda hh, e_idx=e_idx: upT[e_idx].rearrange(
                "(c p) f -> p c f", p=128)[:, :, hh * 512:(hh + 1) * 512]
            act_target = None
            act_target = gu_act(2, gv, uv, True, e_idx)
            dv = lambda pc, e_idx=e_idx: downT[e_idx].rearrange(
                "(c p) f -> p c f", p=128)[:, :, pc * 1024:(pc + 1) * 1024]
            down_phase(8, dv, 2, 1024, first_write=(e_idx == 0))
        gvs = lambda hh: shgT.rearrange("(c p) f -> p c f", p=128)[
            :, :, hh * 512:(hh + 1) * 512]
        uvs = lambda hh: shuT.rearrange("(c p) f -> p c f", p=128)[
            :, :, hh * 512:(hh + 1) * 512]
        act_target = gu_act(4, gvs, uvs, False, 0)
        dvs = lambda pc: shdT.rearrange("(c p) f -> p c f", p=128)[
            :, :, pc * 512:(pc + 1) * 512]
        down_phase(16, dvs, 4, 512, first_write=False)

        sil_cons = []
        for tt in range(4):
            for cc in range(4):
                gci = tt * 4 + cc
                wt_ = [(sra, sra.v)] if gci == 0 else []
                if gci >= 2:
                    wt_.append((vv, sil_cons[gci - 2]))
                P.op("scalar", lambda e, tt=tt, cc=cc: e.activation(
                    tsil[cc % 2][:], routed[:, tt * DM + cc * 512:tt * DM + (cc + 1) * 512],
                    AF.Tanh, scale=0.5), waits=wt_, inc=sc)
                gs_done = sc.v
                w = [(sc, gs_done)]
                if tt >= 1 and cc == 0:
                    w.append((sgd, sgd.v))
                P.op("vector", lambda e, tt=tt, cc=cc: e.scalar_tensor_tensor(
                    v1t[cc % 2][:], tsil[cc % 2][:], 1.0,
                    routed[:, tt * DM + cc * 512:tt * DM + (cc + 1) * 512],
                    ALU.add, ALU.mult), waits=w)
                P.op("vector", lambda e, cc=cc: e.tensor_scalar(
                    gstage[:, cc * 512:(cc + 1) * 512], v1t[cc % 2][:],
                    0.5, None, ALU.mult), inc=vv)
                sil_cons.append(vv.v)
            gsv_done = vv.v
            P.op("sync", lambda e, tt=tt, ss=ss: e.dma_start(
                out=g_dram.ap()[ss * 512 + tt * 128:ss * 512 + (tt + 1) * 128, :],
                in_=gstage[:]), waits=[(vv, gsv_done)], inc=sgd, inc_n=16)
        if DEBUG_TAPS and ss == 0:
            P.op("sync", lambda e: e.dma_start(out=dbg["d_W"], in_=Wt[:]),
                 waits=[(pp, W_done)], inc=ld, inc_n=16)
            P.op("sync", lambda e: e.dma_start(out=dbg["d_routed"], in_=routed[:]),
                 waits=[(sra, sra.v)], inc=ld, inc_n=16)
            P.op("sync", lambda e: e.dma_start(out=dbg["d_act"], in_=act_sb[:]),
                 waits=[(sra, sra.v)], inc=ld, inc_n=16)
        strip_done.append(sgd.v)
    ffn_end = [(sgd, strip_done[-1]), (sra, sra.v)]
    F2.close()

    F4 = ExitStack()
    qsb = lambda n, s, d: F4.enter_context(nc.sbuf_tensor(n, s, d))
    x4 = [qsb(f"x4_{i}", [128, DM], F32) for i in range(2)]
    g4 = [qsb(f"g4_{i}", [128, DM], F32) for i in range(2)]
    y4 = [qsb(f"y4_{i}", [128, DM], F32R) for i in range(2)]
    t4 = qsb("t4", [128, DM], F32)
    s4 = qsb("s4", [128, DM], F32)
    o4 = [qsb(f"o4_{i}", [128, DM], F32) for i in range(2)]
    sout = P.sem("sout")
    sv4 = P.sem("sv4")

    for tt in range(NTILE):
        w = list(ffn_end) + [(cc2, 1)]
        if tt >= 2:
            w.append((sv4, sv4.v - 1))
        P.op("sync", lambda e, tt=tt: e.dma_start(
            out=x4[tt % 2][:], in_=x_own[tt * 128:(tt + 1) * 128, :]),
            waits=w, inc=ld, inc_n=16)
        P.op("sync", lambda e, tt=tt: e.dma_start(
            out=g4[tt % 2][:], in_=g_dram.ap()[tt * 128:(tt + 1) * 128, :]),
            inc=ld, inc_n=16)
        P.op("sync", lambda e, tt=tt: e.dma_start(
            out=y4[tt % 2][:], in_=y_own.ap()[tt * 128:(tt + 1) * 128, :]),
            inc=ld, inc_n=16)
        t_ld = ld.v
        P.op("vector", lambda e, tt=tt: e.tensor_scalar(
            t4[:], x4[tt % 2][:], mcol[:, tt:tt + 1], rcol[:, tt:tt + 1],
            ALU.subtract, ALU.mult), waits=[(ld, t_ld)])
        P.op("vector", lambda e, tt=tt: e.tensor_add(s4[:], t4[:], y4[tt % 2][:]))
        P.op("vector", lambda e, tt=tt: e.tensor_mul(s4[:], s4[:], g4[tt % 2][:]))
        w2 = []
        if tt >= 2:
            w2.append((sout, 16 * (tt - 1)))
        P.op("vector", lambda e, tt=tt: e.tensor_add(
            o4[tt % 2][:], s4[:], x4[tt % 2][:]), waits=w2, inc=sv4)
        oready = sv4.v
        P.op("sync", lambda e, tt=tt: e.dma_start(
            out=out[tt * 128:(tt + 1) * 128, :], in_=o4[tt % 2][:]),
            waits=[(sv4, oready)], inc=sout, inc_n=16)

    if DEBUG_TAPS:
        P.op("gpsimd", lambda e: e.dma_start(out=dbg["d_g"], in_=g_dram.ap()),
             waits=ffn_end, inc=gl, inc_n=16)
        P.op("gpsimd", None, waits=[(gl, gl.v)])
    P.op("sync", None, waits=[(sout, 16 * NTILE), (ld, ld.v)])
    F4.close()

    with ExitStack() as EX:
        blk = EX.enter_context(nc.Block())
        P.emit(nc, blk, EX)
    CX.close()
    return nc


_NC_CACHE = {}


def _host_prep(inputs):
    w_in = np.asarray(inputs["L0_attn_w_in"], np.float32)
    rms_w = np.asarray(inputs["L0_attn_rms_w"], np.float32)
    w_out = np.asarray(inputs["L0_attn_w_out"], np.float32)
    gate_weight = np.asarray(inputs["L0_ffn_gate_weight"], np.float32)
    e_bias = np.asarray(inputs["L0_ffn_e_score_correction_bias"], np.float32)
    gate_w = np.asarray(inputs["L0_ffn_gate_w"], np.float32)
    up_w = np.asarray(inputs["L0_ffn_up_w"], np.float32)
    down_w = np.asarray(inputs["L0_ffn_down_w"], np.float32)
    sh_g = np.asarray(inputs["L0_ffn_sh_gate_w"], np.float32)
    sh_u = np.asarray(inputs["L0_ffn_sh_up_w"], np.float32)
    sh_d = np.asarray(inputs["L0_ffn_sh_down_w"], np.float32)
    x = np.asarray(inputs["L0_attn_x"], np.float32).reshape(S, DM)

    inv = (10000.0 ** (-(np.arange(32, dtype=np.float64) / 32.0))).astype(np.float64)
    ang = np.arange(S, dtype=np.float64)[:, None] * inv[None, :]
    cosT = np.cos(ang).T.astype(np.float32)
    sinT = np.sin(ang).T.astype(np.float32)
    ropeq_h = np.concatenate([np.concatenate([cosT, cosT], 0),
                              np.concatenate([sinT, sinT], 0)], 0) / 8.0
    ropeq_h = ropeq_h.astype(np.float32)
    ropek_h = np.concatenate([cosT, sinT], 0).astype(np.float32)

    ident = np.eye(128, dtype=np.float32)
    dm_ = np.zeros((128, 4, STRIP), np.float32)
    for d in range(4):
        for k in range(128):
            dm_[k, d, :] = (128 * d + k <= np.arange(STRIP)).astype(np.float32)
    dmasks_h = dm_.reshape(128, 4 * STRIP)

    sel = np.zeros((NE, NE * 128), np.float32)
    for e in range(NE):
        sel[e, e * 128:(e + 1) * 128] = 1.0

    common = {
        "rmsw_row": np.ascontiguousarray(np.broadcast_to(rms_w, (128, DM))),
        "ropeq": ropeq_h, "ropek": ropek_h,
        "ident_r": ident, "ident_f": ident,
        "dmasks": dmasks_h,
        "gwT": np.ascontiguousarray(gate_weight.T),
        "ebias": np.ascontiguousarray(np.broadcast_to(e_bias, (128, NE))),
        "selc": sel,
        "gateT": np.ascontiguousarray(gate_w.transpose(0, 2, 1)),
        "upT": np.ascontiguousarray(up_w.transpose(0, 2, 1)),
        "downT": np.ascontiguousarray(down_w.transpose(0, 2, 1)),
        "shgT": np.ascontiguousarray(sh_g.T),
        "shuT": np.ascontiguousarray(sh_u.T),
        "shdT": np.ascontiguousarray(sh_d.T),
    }
    in_maps = []
    for g in range(8):
        heads = [4 * g + j for j in range(HPG)]
        rows = []
        for pair in range(2):
            hA, hB = heads[2 * pair], heads[2 * pair + 1]
            rows += [hA * 64 + d_ for d_ in range(32)]
            rows += [hB * 64 + d_ for d_ in range(32)]
            rows += [hA * 64 + 32 + d_ for d_ in range(32)]
            rows += [hB * 64 + 32 + d_ for d_ in range(32)]
        rows += [2048 + 64 * g + d_ for d_ in range(64)]
        rows += [2560 + 64 * g + d_ for d_ in range(64)]
        w_in_g = np.ascontiguousarray(w_in[rows, :].T)
        w_og_g = np.ascontiguousarray(w_out.T[g * 256:(g + 1) * 256, :])
        m = dict(common)
        m["x_own"] = np.ascontiguousarray(x[g * TOK:(g + 1) * TOK, :])
        m["w_in_gT"] = w_in_g
        m["w_og"] = w_og_g
        in_maps.append(m)
    return in_maps


def kernel(**inputs):
    in_maps = _host_prep(inputs)
    nc = build_program()
    res = run_bass_kernel_spmd(nc, in_maps, core_ids=list(range(8)))
    parts = [np.asarray(res.results[i]["out"], np.float32) for i in range(8)]
    return np.concatenate(parts, axis=0).reshape(1, S, DM)


if __name__ == "__main__":
    nc = build_program()
    print("built ok, n_instructions:", nc.n_instructions())


# version 04
# speedup vs baseline: 1.1113x; 1.1113x over previous
import numpy as np
from contextlib import ExitStack

import concourse.bass as bass
import concourse.mybir as mybir
from concourse.bass_utils import run_bass_kernel_spmd

F32 = mybir.dt.float32
F32R = mybir.dt.float32r
AF = mybir.ActivationFunctionType
ALU = mybir.AluOpType
AX = mybir.AxisListType.X

DM = 2048
S = 8192
TOK = 1024
NCH = 16
STRIP = 512
NSTRIP = 16
HPG = 4
D1 = 65
NE = 16
FMOE = 1024
FSH = 2048
NTILE = 8

ENGINES = ("sync", "gpsimd", "tensor", "vector", "scalar")


class Sem:
    __slots__ = ("name", "v", "h")

    def __init__(self, name):
        self.name = name
        self.v = 0
        self.h = None


class Prog:
    def __init__(self):
        self.ops = []
        self.sems = {}
        self.serialize = False

    def sem(self, name):
        s = self.sems.get(name)
        if s is None:
            s = Sem(name)
            self.sems[name] = s
        return s

    def op(self, engine, fn, waits=(), inc=None, inc_n=1):
        if self.serialize:
            waits = list(waits) + [(s, s.v) for s in self.sems.values()]
        self.ops.append((engine, fn, tuple(waits), inc, inc_n))
        if inc is not None:
            inc.v += inc_n
        return inc.v if inc is not None else None

    def emit(self, nc, block, sx):
        for s in self.sems.values():
            s.h = sx.enter_context(nc.semaphore("sem_" + s.name))
        watermark = {e: {} for e in ENGINES}
        by_engine = {e: [] for e in ENGINES}
        for rec in self.ops:
            by_engine[rec[0]].append(rec)

        def emitter(engine_name):
            def run(eng):
                wm = watermark[engine_name]
                for _, fn, waits, inc, inc_n in by_engine[engine_name]:
                    for (s, v) in waits:
                        if wm.get(s.name, -1) >= v:
                            continue
                        eng.wait_ge(s.h, v)
                        wm[s.name] = v
                    if fn is None:
                        continue
                    inst = fn(eng)
                    if inc is not None:
                        inst.then_inc(inc.h, inc_n)
            return run

        block.sync(emitter("sync"))
        block.gpsimd(emitter("gpsimd"))
        block.tensor(emitter("tensor"))
        block.vector(emitter("vector"))
        block.scalar(emitter("scalar"))


DEBUG_TAPS = False


SERIALIZE = False
TAP_SVAR = True


def build_program():
    nc = bass.Bass("TRN2", target_bir_lowering=False, debug=False)
    P = Prog()
    P.serialize = SERIALIZE

    din = lambda n, s, d: nc.dram_tensor(n, s, d, kind="ExternalInput").ap()
    x_own = din("x_own", [TOK, DM], F32)
    w_in_gT = din("w_in_gT", [DM, 384], F32)
    rmsw_row = din("rmsw_row", [128, DM], F32)
    ropeq = din("ropeq", [128, S], F32)
    ropek = din("ropek", [64, S], F32)
    ident_r = din("ident_r", [128, 128], F32R)
    ident_f = din("ident_f", [128, 128], F32)
    dmasks = din("dmasks", [128, 4 * STRIP], F32R)
    gwTd = din("gwT", [DM, NE], F32)
    ebias = din("ebias", [128, NE], F32)
    selc = din("selc", [NE, NE * 128], F32R)
    gateT = din("gateT", [NE, DM, FMOE], F32)
    upT = din("upT", [NE, DM, FMOE], F32)
    downT = din("downT", [NE, FMOE, DM], F32)
    shgT = din("shgT", [DM, FSH], F32)
    shuT = din("shuT", [DM, FSH], F32)
    shdT = din("shdT", [FSH, DM], F32)
    w_og = din("w_og", [256, DM], F32)

    out = nc.dram_tensor("out", [TOK, DM], F32, kind="ExternalOutput").ap()
    dbg = {}
    if DEBUG_TAPS:
        for nm, shp, dt_ in [("d_z", [DM, TOK], F32R), ("d_xn", [DM, TOK], F32R),
                             ("d_q", [4 * D1, S], F32R), ("d_K", [D1, S], F32R),
                             ("d_V", [128, 64 * D1], F32R), ("d_oloc", [128, 2 * S], F32R),
                             ("d_yp", [S, DM], F32R), ("d_yo", [TOK, DM], F32R),
                             ("d_g", [TOK, DM], F32), ("d_W", [128, 64], F32),
                             ("d_act", [128, 16 * 512], F32R),
                             ("d_mc", [128, 64], F32), ("d_rc", [128, 64], F32),
                             ("d_xnt", [128, DM], F32R), ("d_zt", [128, DM], F32R),
                             ("d_cc", [128, 64], F32), ("d_routed", [128, 4 * DM], F32)]:
            dbg[nm] = nc.dram_tensor(nm, shp, dt_, kind="ExternalOutput").ap()

    z_contrib = nc.dram_tensor("z_contrib", [DM, TOK], F32R)
    z_all = nc.dram_tensor("z_all", [8 * DM, TOK], F32R, addr_space="Shared")
    xn_dram = nc.dram_tensor("xn_dram", [DM, TOK], F32R)
    q_dram = nc.dram_tensor("q_dram", [4 * D1, S], F32R)
    y_part = nc.dram_tensor("y_part", [S, DM], F32R)
    y_own = nc.dram_tensor("y_own", [TOK, DM], F32R)
    g_dram = nc.dram_tensor("g_dram", [TOK, DM], F32)

    CX = ExitStack()
    sbuf = lambda n, s, d: CX.enter_context(nc.sbuf_tensor(n, s, d))

    id_r = sbuf("id_r", [128, 128], F32R)
    id_f = sbuf("id_f", [128, 128], F32)
    ones_row = sbuf("ones_row", [1, 128], F32R)
    eps5 = sbuf("eps5", [128, 1], F32)
    eps6 = sbuf("eps6", [128, 1], F32)
    mcol = sbuf("mcol", [128, 64], F32)
    rcol = sbuf("rcol", [128, 64], F32)

    ld = P.sem("ld")
    gl = P.sem("gl")
    cc1 = P.sem("cc1")
    cc2 = P.sem("cc2")
    vv = P.sem("vv")
    sc = P.sem("sc")
    te = P.sem("te")
    pp = P.sem("pp")

    P.op("sync", lambda e: e.dma_start(out=id_r[:], in_=ident_r), inc=ld, inc_n=16)
    P.op("sync", lambda e: e.dma_start(out=id_f[:], in_=ident_f), inc=ld, inc_n=16)
    consts_ld = ld.v
    P.op("vector", lambda e: e.memset(ones_row[:].bitcast(F32), 1.0))
    P.op("vector", lambda e: e.memset(eps5[:], 1e-5))
    P.op("vector", lambda e: e.memset(eps6[:], 1e-6), inc=vv)
    consts_vv = vv.v

    P1 = ExitStack()
    p1sb = lambda n, s, d: P1.enter_context(nc.sbuf_tensor(n, s, d))
    p1ps = lambda n, s, d: P1.enter_context(nc.psum_tensor(n, s, d))
    rmsw_sb = p1sb("rmsw_sb", [128, DM], F32)
    x_t = [p1sb(f"x_t{i}", [128, DM], F32) for i in range(2)]
    xsq = p1sb("xsq", [128, DM], F32)
    xn_t = p1sb("xn_t", [128, DM], F32R)
    z_t = p1sb("z_t", [128, DM], F32R)
    scol = p1sb("scol", [128, 64], F32)
    s2col = p1sb("s2col", [128, 64], F32)
    sm264 = p1sb("sm264", [128, 64], F32)
    s2n64 = p1sb("s2n64", [128, 64], F32)
    svar64 = p1sb("svar64", [128, 64], F32)
    ln64 = p1sb("ln64", [128, 64], F32)
    rsq64 = p1sb("rsq64", [128, 64], F32)
    rv64 = p1sb("rv64", [128, 64], F32)
    c264 = p1sb("c264", [128, 64], F32)
    ccol = p1sb("ccol", [128, 64], F32)
    vhat64 = p1sb("vhat64", [128, 64], F32)
    rN64 = p1sb("rN64", [128, 64], F32)
    nt64 = p1sb("nt64", [128, 64], F32)
    rpre64 = p1sb("rpre64", [128, 64], F32)
    cpre64 = p1sb("cpre64", [128, 64], F32)
    fm_st = [p1sb(f"fm_st{i}", [128, 512], F32R) for i in range(2)]
    tp_ps = [p1ps(f"tp_ps{i}", [128, 512], F32R) for i in range(2)]

    zdma = P.sem("zdma")
    P.op("sync", lambda e: e.dma_start(out=rmsw_sb[:], in_=rmsw_row), inc=ld, inc_n=16)
    consts_ld = ld.v
    a_done = []
    for tt in range(NTILE):
        buf = x_t[tt % 2]
        w = [(ld, consts_ld)]
        if tt >= 2:
            w.append((vv, a_done[tt - 2]))
        P.op("sync", lambda e, b=buf, tt=tt: e.dma_start(
            out=b[:], in_=x_own[tt * 128:(tt + 1) * 128, :]), waits=w, inc=ld, inc_n=16)
        ld_tile = ld.v
        P.op("vector", lambda e, b=buf: e.tensor_mul(xsq[:], b[:], b[:]),
             waits=[(ld, ld_tile), (vv, consts_vv)])
        P.op("vector", lambda e, b=buf, tt=tt: e.reduce_sum(
            scol[:, tt:tt + 1], b[:], axis=AX))
        P.op("vector", lambda e, tt=tt: e.reduce_sum(
            s2col[:, tt:tt + 1], xsq[:], axis=AX), inc=vv)
        a_done.append(vv.v)
    P.op("gpsimd", lambda e: e.tensor_scalar(
        mcol[:], scol[:], 1.0 / DM, None, ALU.mult), waits=[(vv, a_done[-1])])
    P.op("gpsimd", lambda e: e.tensor_mul(sm264[:], mcol[:], mcol[:]))
    P.op("gpsimd", lambda e: e.tensor_scalar(
        s2n64[:], s2col[:], 1.0 / DM, None, ALU.mult))
    P.op("gpsimd", lambda e: e.tensor_sub(svar64[:], s2n64[:], sm264[:]), inc=pp)
    svar_done = pp.v
    P.op("scalar", lambda e: e.activation(ln64[:], svar64[:], AF.Ln, bias=eps5[:]),
         waits=[(pp, svar_done)])
    P.op("scalar", lambda e: e.activation(rcol[:], ln64[:], AF.Exp, scale=-0.5), inc=sc)
    seed_done = sc.v
    P.op("vector", lambda e: e.tensor_scalar(
        vhat64[:], svar64[:], 1e-5, None, ALU.add), waits=[(sc, seed_done)])
    for step in range(2):
        src = rcol if step == 0 else rN64
        dst = rN64 if step == 0 else rpre64
        P.op("vector", lambda e, src=src: e.tensor_mul(nt64[:], src[:], src[:]))
        P.op("vector", lambda e: e.tensor_mul(nt64[:], nt64[:], vhat64[:]))
        P.op("vector", lambda e: e.tensor_scalar(
            nt64[:], nt64[:], -0.5, 1.5, ALU.mult, ALU.add))
        P.op("vector", lambda e, src=src, dst=dst: e.tensor_mul(
            dst[:], nt64[:], src[:]))
    P.op("vector", lambda e: e.tensor_mul(rsq64[:], rpre64[:], rpre64[:]))
    P.op("vector", lambda e: e.tensor_mul(rv64[:], rsq64[:], svar64[:]))
    P.op("vector", lambda e: e.tensor_scalar(
        vhat64[:], rv64[:], 1e-6, None, ALU.add), inc=vv)
    rv_done = vv.v
    P.op("scalar", lambda e: e.activation(ln64[:], rv64[:], AF.Ln, bias=eps6[:]),
         waits=[(vv, rv_done)])
    P.op("scalar", lambda e: e.activation(c264[:], ln64[:], AF.Exp, scale=-0.5), inc=sc)
    seed2_done = sc.v
    first = True
    for step in range(2):
        src = c264 if step == 0 else rN64
        dst = rN64 if step == 0 else c264
        P.op("vector", lambda e, src=src: e.tensor_mul(nt64[:], src[:], src[:]),
             waits=([(sc, seed2_done)] if first else ()))
        first = False
        P.op("vector", lambda e: e.tensor_mul(nt64[:], nt64[:], vhat64[:]))
        P.op("vector", lambda e: e.tensor_scalar(
            nt64[:], nt64[:], -0.5, 1.5, ALU.mult, ALU.add))
        P.op("vector", lambda e, src=src, dst=dst: e.tensor_mul(
            dst[:], nt64[:], src[:]))
    P.op("vector", lambda e: e.tensor_mul(cpre64[:], c264[:], rpre64[:]), inc=vv)
    nwt_done = vv.v
    P.op("gpsimd", lambda e: e.tensor_copy(rcol[:], rpre64[:]), waits=[(vv, nwt_done)])
    P.op("gpsimd", lambda e: e.tensor_copy(ccol[:], cpre64[:]), inc=pp)
    stats_done = pp.v
    b_done = []
    fm_evac = {}
    n_fm = 0
    for tt in range(NTILE):
        buf = x_t[tt % 2]
        w = [(pp, stats_done)]
        if tt >= 2:
            w.append((vv, b_done[tt - 2]))
        else:
            w.append((vv, a_done[-1]))
        P.op("sync", lambda e, b=buf, tt=tt: e.dma_start(
            out=b[:], in_=x_own[tt * 128:(tt + 1) * 128, :]), waits=w, inc=ld, inc_n=16)
        ld_tile = ld.v
        P.op("vector", lambda e, b=buf, tt=tt: e.tensor_scalar(
            xn_t[:], b[:], mcol[:, tt:tt + 1], rcol[:, tt:tt + 1],
            ALU.subtract, ALU.mult), waits=[(ld, ld_tile), (pp, stats_done)])
        P.op("vector", lambda e, tt=tt: e.scalar_tensor_tensor(
            z_t[:], xn_t[:], ccol[:, tt:tt + 1], rmsw_sb[:], ALU.mult, ALU.mult),
            inc=vv)
        b_done.append(vv.v)
        zx_ready = vv.v
        for half in range(2):
            src = z_t if half == 0 else xn_t
            for grp in range(4):
                pb = tp_ps[n_fm % 2]
                fb = fm_st[n_fm % 2]
                w = [(vv, zx_ready)]
                if n_fm >= 2:
                    w.append((vv, fm_evac[n_fm - 2]))
                for k in range(4):
                    c = grp * 4 + k
                    P.op("tensor", lambda e, pb=pb, src=src, c=c, k=k: e.transpose(
                        pb[:, k * 128:(k + 1) * 128],
                        src[:, c * 128:(c + 1) * 128], id_r[:]),
                        waits=(w if k == 0 else ()), inc=(te if k == 3 else None))
                tp_done = te.v
                w2 = [(te, tp_done)]
                if n_fm >= 2:
                    w2.append((zdma, 16 * (n_fm - 1)))
                P.op("vector", lambda e, pb=pb, fb=fb: e.tensor_copy(fb[:], pb[:]),
                     waits=w2, inc=vv)
                fm_evac[n_fm] = vv.v
                ev_done = vv.v
                dst = z_contrib if half == 0 else xn_dram
                dv = dst.ap().rearrange("(c p) t -> p c t", p=128)
                P.op("sync", lambda e, dv=dv, grp=grp, tt=tt, fb=fb: e.dma_start(
                    out=dv[:, grp * 4:(grp + 1) * 4, tt * 128:(tt + 1) * 128],
                    in_=fb[:].rearrange("p (k t) -> p k t", k=4)),
                    waits=[(vv, ev_done)], inc=zdma, inc_n=16)
                n_fm += 1

    P.op("gpsimd", lambda e: e.collective_compute(
        "AllGather", ALU.bypass, replica_groups=[list(range(8))],
        ins=[z_contrib.ap().opt()], outs=[z_all.ap().opt()]),
        waits=[(zdma, 16 * n_fm)], inc=cc1)
    p1_end = [(zdma, 16 * n_fm), (cc1, 1)]
    if DEBUG_TAPS:
        P.op("sync", lambda e: e.dma_start(out=dbg["d_mc"], in_=mcol[:]),
             waits=[(zdma, 16 * n_fm)], inc=ld, inc_n=16)
        P.op("sync", lambda e: e.dma_start(out=dbg["d_rc"], in_=rcol[:]),
             inc=ld, inc_n=16)
        P.op("sync", lambda e: e.dma_start(out=dbg["d_cc"], in_=ccol[:]),
             inc=ld, inc_n=16)

        P.op("gpsimd", lambda e: e.dma_start(out=dbg["d_z"], in_=z_contrib.ap()),
             waits=[(zdma, 16 * n_fm)], inc=gl, inc_n=16)
        P.op("gpsimd", lambda e: e.dma_start(out=dbg["d_xn"], in_=xn_dram.ap()),
             inc=gl, inc_n=16)
    P1.close()

    A23 = ExitStack()
    Kt = A23.enter_context(nc.sbuf_tensor("Kt", [D1, S], F32R))
    Vt = A23.enter_context(nc.sbuf_tensor("Vt", [128, 64 * D1], F32R))

    KQ = ExitStack()
    ksb = lambda n, s, d: KQ.enter_context(nc.sbuf_tensor(n, s, d))
    kps = lambda n, s, d: KQ.enter_context(nc.psum_tensor(n, s, d))
    z_strip = [ksb(f"z_strip{i}", [128, NCH * STRIP], F32R) for i in range(2)]
    winsb = ksb("winsb", [128, NCH * 384], F32R)
    rq_t = [ksb(f"rq_t{i}", [128, STRIP], F32) for i in range(2)]
    rk_t = [ksb(f"rk_t{i}", [64, STRIP], F32) for i in range(2)]
    stagA = ksb("stagA", [64, STRIP], F32R)
    stagB = ksb("stagB", [64, STRIP], F32R)
    tmpA = ksb("tmpA", [64, STRIP], F32)
    tmpB = ksb("tmpB", [64, STRIP], F32)
    v_stage = ksb("v_stage", [64, STRIP], F32R)
    qstage = ksb("qstage", [64, HPG * STRIP], F32R)
    q_ps = [kps(f"q_ps{i}", [128, STRIP], F32) for i in range(4)]
    kv_psl = [kps(f"kv_ps{i}", [128, STRIP], F32) for i in range(2)]
    vt_ps = kps("vt_ps", [128, 4 * 64], F32R)

    P.op("vector", lambda e: e.memset(Kt[64:65, :].bitcast(F32), 1.0), waits=p1_end)
    P.op("vector", lambda e: e.memset(
        Vt[:].bitcast(F32).rearrange("p (k d) -> p k d", k=64)[:, :, 64:65].opt(), 1.0),
        inc=vv)
    kvinit = vv.v
    wv = w_in_gT.rearrange("(c p) b -> p c b", p=128)
    P.op("gpsimd", lambda e: e.dma_start(
        out=winsb[:].rearrange("p (c b) -> p c b", b=384)[:, 0:8, :],
        in_=wv[:, 0:8, :]), waits=p1_end, inc=gl, inc_n=16)
    P.op("gpsimd", lambda e: e.dma_start(
        out=winsb[:].rearrange("p (c b) -> p c b", b=384)[:, 8:16, :],
        in_=wv[:, 8:16, :]), inc=gl, inc_n=16)
    win_ld = gl.v

    zav = z_all.ap().rearrange("(r c p) t -> r p c t", r=8, p=128)
    rope_done = []
    qdma = P.sem("qdma")
    for s in range(NSTRIP):
        r, ss = s // 2, s % 2
        zb = z_strip[s % 2]
        w = list(p1_end)
        if s >= 2:
            w.append((vv, rope_done[s - 2]))
        for piece in range(2):
            P.op("sync", lambda e, zb=zb, r=r, ss=ss, piece=piece: e.dma_start(
                out=zb[:].rearrange("p (c t) -> p c t", c=NCH)[:, piece * 8:(piece + 1) * 8, :],
                in_=zav[r, :, piece * 8:(piece + 1) * 8, ss * 512:(ss + 1) * 512]),
                waits=(w if piece == 0 else ()), inc=ld, inc_n=16)
        P.op("sync", lambda e, s=s, b=rq_t[s % 2]: e.dma_start(
            out=b[:], in_=ropeq[:, s * 512:(s + 1) * 512]), inc=ld, inc_n=16)
        P.op("sync", lambda e, s=s, b=rk_t[s % 2]: e.dma_start(
            out=b[:], in_=ropek[:, s * 512:(s + 1) * 512]), inc=ld, inc_n=16)
        z_ld = ld.v
        w = [(ld, z_ld), (gl, win_ld)]
        if s >= 2:
            w.append((vv, rope_done[s - 2]))
        kv_ps = kv_psl[s % 2]
        for c in range(NCH):
            for b in range(3):
                ps = q_ps[(s % 2) * 2 + b] if b < 2 else kv_ps
                P.op("tensor", lambda e, ps=ps, c=c, b=b, zb=zb: e.matmul(
                    ps[:], winsb[:, c * 384 + b * 128:c * 384 + (b + 1) * 128],
                    zb[:, c * 512:(c + 1) * 512],
                    start=(c == 0), stop=(c == NCH - 1)),
                    waits=(w if (c == 0 and b == 0) else ()),
                    inc=(te if (c == NCH - 1 and b == 2) else None))
        mm_done = te.v
        for b in range(2):
            qp = q_ps[(s % 2) * 2 + b]
            rqc = rq_t[s % 2]
            wq = [(te, mm_done), (vv, kvinit), (pp, pp.v)]
            P.op("vector", lambda e, qp=qp, rqc=rqc: e.tensor_mul(
                tmpA[:], qp[0:64, :], rqc[0:64, :]), waits=wq)
            P.op("vector", lambda e, qp=qp, rqc=rqc: e.tensor_mul(
                tmpB[:], qp[64:128, :], rqc[64:128, :]))
            P.op("vector", lambda e: e.tensor_sub(stagA[:], tmpA[:], tmpB[:]))
            P.op("vector", lambda e, qp=qp, rqc=rqc: e.tensor_mul(
                tmpA[:], qp[64:128, :], rqc[0:64, :]))
            P.op("vector", lambda e, qp=qp, rqc=rqc: e.tensor_mul(
                tmpB[:], qp[0:64, :], rqc[64:128, :]))
            P.op("vector", lambda e: e.tensor_add(stagB[:], tmpA[:], tmpB[:]), inc=vv)
            st_ready = vv.v
            for idx, (srcT, srclo, hd, dstlo) in enumerate((
                    (stagA, 0, 2 * b, 0), (stagA, 32, 2 * b + 1, 0),
                    (stagB, 0, 2 * b, 32), (stagB, 32, 2 * b + 1, 32))):
                P.op("gpsimd", lambda e, srcT=srcT, srclo=srclo, hd=hd, dstlo=dstlo:
                     e.tensor_copy(
                         qstage[dstlo:dstlo + 32, hd * 512:(hd + 1) * 512],
                         srcT[srclo:srclo + 32, :]),
                     waits=[(vv, st_ready)], inc=(pp if idx == 3 else None))
        pool_done = pp.v
        rkc = rk_t[s % 2]
        P.op("vector", lambda e, rkc=rkc, kv=kv_ps: e.tensor_mul(
            tmpA[0:32, :], kv[0:32, :], rkc[0:32, :]), waits=[(te, mm_done)])
        P.op("vector", lambda e, rkc=rkc, kv=kv_ps: e.tensor_mul(
            tmpB[0:32, :], kv[32:64, :], rkc[32:64, :]))
        P.op("vector", lambda e, s=s: e.tensor_sub(
            Kt[0:32, s * 512:(s + 1) * 512], tmpA[0:32, :], tmpB[0:32, :]))
        P.op("vector", lambda e, rkc=rkc, kv=kv_ps: e.tensor_mul(
            tmpA[0:32, :], kv[32:64, :], rkc[0:32, :]))
        P.op("vector", lambda e, rkc=rkc, kv=kv_ps: e.tensor_mul(
            tmpB[0:32, :], kv[0:32, :], rkc[32:64, :]))
        P.op("vector", lambda e, s=s: e.tensor_add(
            Kt[32:64, s * 512:(s + 1) * 512], tmpA[0:32, :], tmpB[0:32, :]))
        P.op("vector", lambda e, kv=kv_ps: e.tensor_copy(v_stage[:], kv[64:128, :]), inc=vv)
        vst_ready = vv.v
        for k in range(4):
            P.op("tensor", lambda e, k=k: e.transpose(
                vt_ps[:, k * 64:(k + 1) * 64], v_stage[:, k * 128:(k + 1) * 128],
                id_r[0:64, 0:64]), waits=([(vv, vst_ready)] if k == 0 else ()),
                inc=(te if k == 3 else None))
        vtp_done = te.v
        for k in range(4):
            kt_idx = 4 * s + k
            P.op("vector", lambda e, k=k, kt_idx=kt_idx: e.tensor_copy(
                Vt[:, kt_idx * D1:kt_idx * D1 + 64], vt_ps[:, k * 64:(k + 1) * 64]),
                waits=[(te, vtp_done)], inc=(vv if k == 3 else None))
        rope_done.append(vv.v)
        qdv = q_dram.ap().rearrange("(j p) t -> p j t", p=D1)
        P.op("sync", lambda e, s=s, qdv=qdv: e.dma_start(
            out=qdv[0:64, :, s * 512:(s + 1) * 512],
            in_=qstage[:].rearrange("p (j t) -> p j t", j=4)),
            waits=[(pp, pool_done)], inc=qdma, inc_n=16)

    kvq_end = [(qdma, 16 * NSTRIP), (vv, rope_done[-1])]
    if DEBUG_TAPS:
        P.op("gpsimd", lambda e: e.dma_start(out=dbg["d_q"], in_=q_dram.ap()),
             waits=kvq_end, inc=gl, inc_n=16)
        P.op("sync", lambda e: e.dma_start(out=dbg["d_K"], in_=Kt[:]),
             waits=kvq_end, inc=ld, inc_n=16)
        P.op("sync", lambda e: e.dma_start(out=dbg["d_V"], in_=Vt[:]),
             waits=kvq_end, inc=ld, inc_n=16)
    KQ.close()

    A3 = ExitStack()
    asb = lambda n, s, d: A3.enter_context(nc.sbuf_tensor(n, s, d))
    aps = lambda n, s, d: A3.enter_context(nc.psum_tensor(n, s, d))
    o_loc = A23.enter_context(nc.sbuf_tensor("o_loc", [128, 2 * S], F32R))
    qh_t = asb("qh", [D1, S], F32R)
    qh = [qh_t, qh_t]
    qabs_t = asb("qabs_t", [64, STRIP], F32R)
    kmax_f = asb("kmax_f", [64, 1], F32)
    kmax_r = asb("kmax_r", [64, 1], F32R)
    Pt = [asb(f"Pt{i}", [128, STRIP], F32R) for i in range(6)]
    mask_sb = asb("mask_sb", [128, 4 * STRIP], F32R)
    rl_f = asb("rl_f", [1, STRIP], F32)
    rl_f2 = asb("rl_f2", [1, STRIP], F32)
    rl_t = asb("rl_t", [1, STRIP], F32)
    rl_sb = asb("rl_sb", [1, STRIP], F32R)
    o_stage = asb("o_stage", [64, STRIP], F32R)
    S_ps = [aps(f"S_ps{i}", [128, STRIP], F32) for i in range(4)]
    o_ps = [aps(f"o_ps{i}", [D1, STRIP], F32) for i in range(2)]
    bnd_ps = [aps(f"bnd_ps{i}", [1, STRIP], F32) for i in range(1)]
    rb_ps = aps("rb_ps", [64, STRIP], F32)

    P.op("sync", lambda e: e.dma_start(out=mask_sb[:], in_=dmasks),
         waits=kvq_end, inc=ld, inc_n=16)
    mask_ld = ld.v
    P.op("vector", lambda e: e.reduce_max(
        kmax_f[:], Kt[0:64, :], axis=AX, apply_absolute_value=True),
        waits=kvq_end, inc=vv)
    P.op("gpsimd", lambda e: e.tensor_copy(kmax_r[:], kmax_f[:]),
         waits=[(vv, vv.v)], inc=pp)
    kmax_sem, kmax_ready = pp, pp.v

    sqk = P.sem("sqk")
    sxp = P.sem("sxp")
    smk = P.sem("smk")
    sav = P.sem("sav")
    slr = P.sem("slr")
    srb = P.sem("srb")
    son = P.sem("son")
    sqd = P.sem("sqd")
    smr = P.sem("smr")

    qdv2 = q_dram.ap().rearrange("(j p) t -> j p t", p=D1)
    it = 0
    nmask = 0
    pend_av = []
    j_end_av = {}

    def emit_av(rec):
        (kt, i, masked, nm, jj, is_first, is_last, obank) = rec
        w = [(smk, nm)] if masked else [(sxp, i + 1)]
        if is_first and jj >= 2:
            w.append((son, jj - 1))
        P.op("tensor", lambda e, kt=kt, i=i, is_first=is_first, is_last=is_last,
             obank=obank: e.matmul(
                 o_ps[obank][:], Vt[:, kt * D1:(kt + 1) * D1], Pt[i % 6][:],
                 start=is_first, stop=is_last), waits=w, inc=sav)

    for h in range(HPG):
        w = [(qdma, 16 * NSTRIP)]
        if h >= 1:
            w.append((sav, j_end_av[(h - 1) * 16 + 15]))
            w.append((son, h * 16))
        P.op("sync", lambda e, h=h: e.dma_start(
            out=qh[h % 2][0:64, :], in_=qdv2[h, 0:64, :]), waits=w, inc=sqd, inc_n=16)
        q_ld = sqd.v
        for qs in range(NSTRIP):
            P.op("scalar", lambda e, h=h, qs=qs: e.activation(
                qabs_t[:], qh[h % 2][0:64, qs * 512:(qs + 1) * 512], AF.Abs),
                waits=[(sqd, q_ld), (te, te.v)], inc=sc)
            qa_ready = sc.v
            bb = bnd_ps[0]
            wb = [(sc, qa_ready), (kmax_sem, kmax_ready)]
            if h * 16 + qs >= 1:
                wb.append((smr, h * 16 + qs))
            P.op("tensor", lambda e, bb=bb: e.matmul(
                bb[:], kmax_r[:], qabs_t[:], start=True, stop=True),
                waits=wb, inc=te)
            bmm = te.v
            P.op("vector", lambda e, h=h, qs=qs, bb=bb: e.tensor_scalar(
                qh[h % 2][64:65, qs * 512:(qs + 1) * 512], bb[0:1, :],
                -1.0, -1.0, ALU.mult, ALU.add), waits=[(te, bmm)], inc=smr)
        mrow_base = smr.v - NSTRIP
        for qs in range(NSTRIP):
            jj = h * 16 + qs
            nkt = 4 * qs + 4
            obank = jj % 2
            for kt in range(nkt):
                i = it
                masked = kt >= 4 * qs
                w = [(smr, mrow_base + qs + 1)]
                if i >= 4:
                    w.append((sxp, i - 3))
                P.op("tensor", lambda e, kt=kt, i=i, h=h, qs=qs: e.matmul(
                    S_ps[i % 4][:], Kt[:, kt * 128:(kt + 1) * 128],
                    qh[h % 2][:, qs * 512:(qs + 1) * 512],
                    start=True, stop=True), waits=w, inc=sqk)
                w = [(sqk, i + 1), (ld, mask_ld)]
                if i >= 6:
                    w.append((sav, i - 5))
                P.op("scalar", lambda e, i=i: e.activation(
                    Pt[i % 6][:], S_ps[i % 4][:], AF.Exp), waits=w, inc=sxp)
                if masked:
                    d = kt - 4 * qs
                    nmask += 1
                    P.op("vector", lambda e, i=i, d=d: e.tensor_mul(
                        Pt[i % 6][:], Pt[i % 6][:],
                        mask_sb[:, d * 512:(d + 1) * 512]),
                        waits=[(sxp, i + 1)], inc=smk)
                pend_av.append((kt, i, masked, nmask, jj, kt == 0, kt == nkt - 1, obank))
                if len(pend_av) >= 3:
                    emit_av(pend_av.pop(0))
                it += 1
            while pend_av:
                emit_av(pend_av.pop(0))
            j_end_av[jj] = sav.v
            P.op("scalar", lambda e, obank=obank: e.activation(
                rl_f[:], o_ps[obank][64:65, :], AF.Ln), waits=[(sav, sav.v)])
            P.op("scalar", lambda e: e.activation(
                rl_f2[:], rl_f[:], AF.Exp, scale=-1.0), inc=sc)
            rl_seed = sc.v
            P.op("vector", lambda e, obank=obank: e.tensor_mul(
                rl_t[:], rl_f2[:], o_ps[obank][64:65, :]), waits=[(sc, rl_seed)])
            P.op("vector", lambda e: e.tensor_scalar(
                rl_t[:], rl_t[:], -1.0, 2.0, ALU.mult, ALU.add))
            P.op("vector", lambda e: e.tensor_mul(
                rl_sb[:], rl_t[:], rl_f2[:]), inc=slr)
            lr_done = slr.v
            w = [(slr, lr_done)]
            if jj >= 1:
                w.append((son, jj))
            P.op("tensor", lambda e: e.matmul(
                rb_ps[:], ones_row[0:1, 0:64], rl_sb[:], start=True, stop=True),
                waits=w, inc=srb)
            rb_done = srb.v
            P.op("vector", lambda e, obank=obank: e.tensor_copy(
                o_stage[:], o_ps[obank][0:64, :]), waits=[(srb, rb_done)])
            c2 = h // 2
            plo = (h % 2) * 64
            P.op("vector", lambda e, c2=c2, plo=plo, qs=qs: e.tensor_mul(
                o_loc[plo:plo + 64, c2 * S + qs * 512:c2 * S + (qs + 1) * 512],
                o_stage[:], rb_ps[:]), inc=son)

    attn_done = [(son, son.v)]
    if DEBUG_TAPS:
        P.op("sync", lambda e: e.dma_start(out=dbg["d_oloc"], in_=o_loc[:]),
             waits=attn_done, inc=ld, inc_n=16)
    A3.close()

    WO = ExitStack()
    wsb = lambda n, s, d: WO.enter_context(nc.sbuf_tensor(n, s, d))
    wps = lambda n, s, d: WO.enter_context(nc.psum_tensor(n, s, d))
    wog_sb = wsb("wog_sb", [128, 2 * DM], F32R)
    y_stage = [wsb(f"y_stage{i}", [128, STRIP], F32R) for i in range(4)]
    yw_ps = [wps(f"yw_ps{i}", [128, STRIP], F32) for i in range(2)]

    P.op("gpsimd", lambda e: e.dma_start(
        out=wog_sb[:].rearrange("p (c n) -> p c n", c=2),
        in_=w_og.rearrange("(c p) n -> p c n", p=128)), waits=attn_done,
        inc=gl, inc_n=16)
    wog_ld = gl.v
    syd = P.sem("syd")
    syv = P.sem("syv")
    n_yd = 0
    for tt64 in range(64):
        for ds in range(4):
            gidx = tt64 * 4 + ds
            bank = yw_ps[gidx % 2]
            w = [(gl, wog_ld)] + attn_done
            if gidx >= 2:
                w.append((syv, gidx - 1))
            for c2 in range(2):
                P.op("tensor", lambda e, bank=bank, c2=c2, tt64=tt64, ds=ds: e.matmul(
                    bank[:], o_loc[:, c2 * S + tt64 * 128:c2 * S + (tt64 + 1) * 128],
                    wog_sb[:, c2 * DM + ds * 512:c2 * DM + (ds + 1) * 512],
                    start=(c2 == 0), stop=(c2 == 1)),
                    waits=(w if c2 == 0 else ()), inc=(te if c2 == 1 else None))
            ymm = te.v
            w2 = [(te, ymm)]
            if gidx >= 4:
                w2.append((syd, 16 * (gidx - 3)))
            P.op("vector", lambda e, bank=bank, gidx=gidx: e.tensor_copy(
                y_stage[gidx % 4][:], bank[:]), waits=w2, inc=syv)
            yv = syv.v
            cblk = (tt64 % 8) // 2
            row0 = cblk * 2048 + (tt64 // 8) * 256 + (tt64 % 2) * 128
            P.op("sync", lambda e, row0=row0, ds=ds, gidx=gidx: e.dma_start(
                out=y_part.ap()[row0:row0 + 128, ds * 512:(ds + 1) * 512],
                in_=y_stage[gidx % 4][:]), waits=[(syv, yv)], inc=syd, inc_n=16)
            n_yd += 1

    rs_ready = (syd, 16 * n_yd)
    wo_end = [(syd, 16 * n_yd), (syv, syv.v)]
    if DEBUG_TAPS:
        P.op("gpsimd", lambda e: e.dma_start(out=dbg["d_yp"], in_=y_part.ap()),
             waits=wo_end, inc=gl, inc_n=16)
        P.op("gpsimd", lambda e: e.dma_start(out=dbg["d_yo"], in_=y_own.ap()),
             waits=[(cc2, 1)], inc=gl, inc_n=16)
    WO.close()
    A23.close()

    F2 = ExitStack()
    fsb = lambda n, s, d: F2.enter_context(nc.sbuf_tensor(n, s, d))
    fps = lambda n, s, d: F2.enter_context(nc.psum_tensor(n, s, d))
    xn_s = fsb("xn_s", [128, NCH * STRIP], F32R)
    wslot = [fsb(f"wslot{i}", [128, 8 * 1024], F32R) for i in range(2)]
    act_sb = fsb("act_sb", [128, 16 * 512], F32R)
    routed = fsb("routed", [128, 4 * DM], F32)
    gw_sb = fsb("gw_sb", [128, NCH * NE], F32R)
    eb_sb = fsb("eb_sb", [128, NE], F32)
    sel_sb = fsb("sel_sb", [NE, 128], F32R)
    lg_sb = fsb("lg_sb", [16, STRIP], F32)
    ltok = fsb("ltok", [128, 64], F32)
    th_t = fsb("th_t", [128, 64], F32)
    scrs = fsb("scrs", [128, 64], F32)
    s4c = fsb("s4c", [128, 64], F32)
    pair_t = fsb("pair_t", [128, 24], F32)
    mx_t = fsb("mx_t", [128, 20], F32)
    mxl = fsb("mxl", [128, 16], F32)
    gs_l = fsb("gs_l", [128, 4], F32)
    tp1 = fsb("tp1", [128, 16], F32)
    tp2 = fsb("tp2", [128, 16], F32)
    gselt = fsb("gselt", [128, 4], F32)
    pen_t = fsb("pen_t", [128, 4], F32)
    sec_t = fsb("sec_t", [128, 1], F32)
    mskd = fsb("mskd", [128, 16], F32)
    mxt = fsb("mxt", [128, 1], F32)
    Wt = fsb("Wt", [128, 64], F32)
    rs_t = fsb("rs_t", [128, 1], F32)
    Wfm = fsb("Wfm", [16, STRIP], F32R)
    tsil = [fsb(f"tsil{i}", [128, STRIP], F32) for i in range(2)]
    v1t = [fsb(f"v1t{i}", [128, STRIP], F32) for i in range(2)]
    gstage = fsb("gstage", [128, DM], F32)
    gu_ps = [fps(f"gu_ps{i}", [128, STRIP], F32) for i in range(4)]
    wb_ps = fps("wb_ps", [128, STRIP], F32)
    y_ps = [fps(f"y_ps{i}", [128, STRIP], F32) for i in range(2)]
    lg_ps = fps("lg_ps", [16, STRIP], F32)

    P.op("gpsimd", lambda e: e.dma_start(
        out=gw_sb[:].rearrange("p (c n) -> p c n", c=NCH),
        in_=gwTd.rearrange("(c p) n -> p c n", p=128)), waits=wo_end, inc=gl, inc_n=16)
    P.op("sync", lambda e: e.dma_start(out=eb_sb[:], in_=ebias), waits=wo_end,
         inc=ld, inc_n=16)
    f2_static = [(gl, gl.v), (ld, ld.v)]
    ssel = P.sem("ssel")

    xnv = xn_dram.ap().rearrange("(c p) t -> p c t", p=128)
    swl = P.sem("swl")
    swf = P.sem("swf")
    sas = P.sem("sas")
    sra = P.sem("sra")
    sgd = P.sem("sgd")
    svh = P.sem("svh")
    wctr = [0]

    def wload(in3, waits=()):
        slot = wslot[wctr[0] % 2]
        A = in3.shape[1]
        B = in3.shape[2]
        w = list(waits)
        if wctr[0] >= 2:
            w.append((swf, wctr[0] - 1))
        assert A * B == 8 * 1024
        P.op("gpsimd", lambda e, slot=slot, in3=in3, A=A: e.dma_start(
            out=slot[:].rearrange("p (a b) -> p a b", a=A),
            in_=in3), waits=w, inc=swl, inc_n=16)
        wctr[0] += 1
        return slot, swl.v

    strip_done = []
    for ss in range(2):
        w = list(f2_static)
        if ss > 0:
            w.append((sgd, strip_done[-1]))
        for piece in range(2):
            P.op("sync", lambda e, ss=ss, piece=piece: e.dma_start(
                out=xn_s[:].rearrange("p (c t) -> p c t", c=NCH)[:, piece * 8:(piece + 1) * 8, :],
                in_=xnv[:, piece * 8:(piece + 1) * 8, ss * 512:(ss + 1) * 512]),
                waits=(w if piece == 0 else ()), inc=ld, inc_n=16)
        xn_ld = ld.v
        for c in range(NCH):
            P.op("tensor", lambda e, c=c: e.matmul(
                lg_ps[:], gw_sb[:, c * NE:(c + 1) * NE],
                xn_s[:, c * 512:(c + 1) * 512],
                start=(c == 0), stop=(c == NCH - 1)),
                waits=([(ld, xn_ld)] + ([(vv, vv.v), (pp, pp.v)] if ss > 0 else []))
                if c == 0 else (), inc=(te if c == NCH - 1 else None))
        lgmm = te.v
        P.op("vector", lambda e: e.tensor_copy(lg_sb[:], lg_ps[:]),
             waits=[(te, lgmm)], inc=vv)
        lgev = vv.v
        for k in range(4):
            P.op("tensor", lambda e, k=k: e.transpose(
                y_ps[0][:, k * 16:(k + 1) * 16],
                lg_sb[0:16, k * 128:(k + 1) * 128], id_f[0:16, 0:16]),
                waits=([(vv, lgev)] if k == 0 else ()),
                inc=(te if k == 3 else None))
        lgtp = te.v
        P.op("vector", lambda e: e.tensor_copy(ltok[:], y_ps[0][:, 0:64]),
             waits=[(te, lgtp)], inc=vv)
        ltok_done = vv.v
        P.op("scalar", lambda e: e.activation(th_t[:], ltok[:], AF.Tanh, scale=0.5),
             waits=[(vv, ltok_done)], inc=sc)
        th_done = sc.v
        P.op("vector", lambda e: e.tensor_scalar(
            scrs[:], th_t[:], 0.5, 0.5, ALU.mult, ALU.add),
            waits=[(sc, th_done), (ld, f2_static[1][1])], inc=vv)
        scrs_done = vv.v
        for k in range(4):
            P.op("gpsimd", lambda e, k=k: e.tensor_add(
                s4c[:, k * 16:(k + 1) * 16], scrs[:, k * 16:(k + 1) * 16], eb_sb[:]),
                waits=([(vv, scrs_done)] if k == 0 else ()))
        for k in range(4):
            b0 = k * 16
            vw = lambda i, b0=b0: s4c[:, b0:b0 + 16].rearrange(
                "p (g e) -> p g e", g=4)[:, :, i:i + 1].opt()
            pairs = [(0, 1), (2, 3), (0, 2), (1, 3), (0, 3), (1, 2)]
            for pi, (a, bq) in enumerate(pairs):
                P.op("gpsimd", lambda e, a=a, bq=bq, pi=pi, vw=vw: e.tensor_add(
                    pair_t[:, pi * 4:(pi + 1) * 4], vw(a), vw(bq)),
                    inc=(pp if pi == 5 else None))
            P.op("vector", lambda e: e.tensor_max(
                mx_t[:, 0:4], pair_t[:, 0:4], pair_t[:, 4:8]), waits=[(pp, pp.v)])
            P.op("vector", lambda e: e.tensor_max(
                mx_t[:, 4:8], pair_t[:, 8:12], pair_t[:, 12:16]))
            P.op("vector", lambda e: e.tensor_max(
                mx_t[:, 8:12], pair_t[:, 16:20], pair_t[:, 20:24]), inc=vv)
            P.op("gpsimd", lambda e: e.tensor_copy(mxl[:, 0:12], mx_t[:, 0:12]),
                 waits=[(vv, vv.v)], inc=pp)
            P.op("vector", lambda e: e.tensor_max(
                mx_t[:, 12:16], mxl[:, 0:4], mxl[:, 4:8]), waits=[(pp, pp.v)], inc=vv)
            P.op("gpsimd", lambda e: e.tensor_copy(mxl[:, 12:16], mx_t[:, 12:16]),
                 waits=[(vv, vv.v)], inc=pp)
            P.op("vector", lambda e: e.tensor_max(
                mx_t[:, 16:20], mxl[:, 12:16], mxl[:, 8:12]), waits=[(pp, pp.v)], inc=vv)
            P.op("gpsimd", lambda e: e.tensor_copy(gs_l[:], mx_t[:, 16:20]),
                 waits=[(vv, vv.v)], inc=pp)
            P.op("vector", lambda e: e.tensor_max(
                mx_t[:, 0:1], gs_l[:, 0:1], gs_l[:, 1:2]), waits=[(pp, pp.v)])
            P.op("vector", lambda e: e.tensor_max(
                mx_t[:, 1:2], gs_l[:, 2:3], gs_l[:, 3:4]), inc=vv)
            P.op("gpsimd", lambda e: e.tensor_copy(mxl[:, 0:2], mx_t[:, 0:2]),
                 waits=[(vv, vv.v)], inc=pp)
            P.op("vector", lambda e: e.tensor_max(
                mx_t[:, 2:3], gs_l[:, 1:2], mxl[:, 1:2]), waits=[(pp, pp.v)])
            P.op("vector", lambda e: e.tensor_max(
                mx_t[:, 3:4], gs_l[:, 0:1], mxl[:, 1:2]))
            P.op("vector", lambda e: e.tensor_max(
                mx_t[:, 4:5], gs_l[:, 3:4], mxl[:, 0:1]))
            P.op("vector", lambda e: e.tensor_max(
                mx_t[:, 5:6], gs_l[:, 2:3], mxl[:, 0:1]), inc=vv)
            P.op("gpsimd", lambda e: e.tensor_copy(mxl[:, 2:6], mx_t[:, 2:6]),
                 waits=[(vv, vv.v)], inc=pp)
            P.op("vector", lambda e: e.tensor_tensor(
                mx_t[:, 6:7], mxl[:, 2:3], mxl[:, 3:4], ALU.min), waits=[(pp, pp.v)])
            P.op("vector", lambda e: e.tensor_tensor(
                mx_t[:, 7:8], mxl[:, 4:5], mxl[:, 5:6], ALU.min), inc=vv)
            P.op("gpsimd", lambda e: e.tensor_copy(mxl[:, 6:8], mx_t[:, 6:8]),
                 waits=[(vv, vv.v)], inc=pp)
            P.op("vector", lambda e: e.tensor_tensor(
                mx_t[:, 8:9], mxl[:, 6:7], mxl[:, 7:8], ALU.min),
                waits=[(pp, pp.v)], inc=vv)
            P.op("gpsimd", lambda e: e.tensor_copy(sec_t[:], mx_t[:, 8:9]),
                 waits=[(vv, vv.v)])
            P.op("gpsimd", lambda e: e.tensor_single_scalar(
                gselt[:], gs_l[:], sec_t[:], ALU.is_ge))
            P.op("gpsimd", lambda e: e.tensor_scalar(
                pen_t[:], gselt[:], 1e30, -1e30, ALU.mult, ALU.add))
            for j in range(4):
                P.op("gpsimd", lambda e, j=j, b0=b0: e.tensor_single_scalar(
                    mskd[:, j * 4:(j + 1) * 4], s4c[:, b0 + j * 4:b0 + (j + 1) * 4],
                    pen_t[:, j:j + 1], ALU.add),
                    inc=(pp if j == 3 else None))
            for itk in range(4):
                P.op("vector", lambda e: e.reduce_max(mxt[:], mskd[:], axis=AX),
                     waits=[(pp, pp.v)], inc=vv)
                P.op("gpsimd", lambda e: e.tensor_single_scalar(
                    tp1[:, 0:16], mskd[:], mxt[:], ALU.is_ge),
                    waits=[(vv, vv.v)], inc=pp)
                P.op("vector", lambda e, b0=b0: e.tensor_mul(
                    tp2[:, 0:16], scrs[:, b0:b0 + 16], tp1[:, 0:16]),
                    waits=[(pp, pp.v)], inc=vv)
                if itk == 0:
                    P.op("gpsimd", lambda e, b0=b0: e.tensor_copy(
                        Wt[:, b0:b0 + 16], tp2[:, 0:16]), waits=[(vv, vv.v)])
                else:
                    P.op("gpsimd", lambda e, b0=b0: e.tensor_add(
                        Wt[:, b0:b0 + 16], Wt[:, b0:b0 + 16], tp2[:, 0:16]),
                        waits=[(vv, vv.v)])
                P.op("gpsimd", lambda e: e.tensor_scalar(
                    gselt[:, 0:4].bitcast(F32) if False else tp2[:, 0:16],
                    tp1[:, 0:16], -2e30, None, ALU.mult))
                P.op("gpsimd", lambda e: e.tensor_add(
                    mskd[:], mskd[:], tp2[:, 0:16]), inc=pp)
            P.op("vector", lambda e, b0=b0: e.reduce_sum(
                rs_t[:], Wt[:, b0:b0 + 16], axis=AX), waits=[(pp, pp.v)], inc=vv)
            P.op("gpsimd", lambda e: e.tensor_single_scalar(
                sec_t[:], rs_t[:], 1e-20, ALU.add), waits=[(vv, vv.v)], inc=pp)
            P.op("vector", lambda e: e.reciprocal(rs_t[:], sec_t[:]),
                 waits=[(pp, pp.v)], inc=vv)
            P.op("gpsimd", lambda e, b0=b0: e.tensor_scalar(
                Wt[:, b0:b0 + 16], Wt[:, b0:b0 + 16], rs_t[:], None, ALU.mult),
                waits=[(vv, vv.v)], inc=pp)
        W_done = pp.v
        for k in range(4):
            P.op("tensor", lambda e, k=k: e.transpose(
                wb_ps[0:16, k * 128:(k + 1) * 128],
                Wt[:, k * 16:(k + 1) * 16], id_f[:]),
                waits=([(pp, W_done)] if k == 0 else ()), inc=(te if k == 3 else None))
        wtp = te.v
        P.op("vector", lambda e: e.tensor_scalar(
            Wfm[:], wb_ps[0:16, :], 1.25, None, ALU.mult),
            waits=[(te, wtp)], inc=vv)
        wfm_done = vv.v

        def gu_act(nhalf, gview, uview, use_wb, e_idx):
            for hh in range(nhalf):
                gs_, glc = wload(gview(hh), waits=[(ld, xn_ld)])
                us_, ulc = wload(uview(hh))
                for fc in range(4):
                    fcg = hh * 4 + fc
                    gb = gu_ps[(fcg % 2) * 2]
                    ub = gu_ps[(fcg % 2) * 2 + 1]
                    w = [(swl, ulc)]
                    if use_wb:
                        w.append((te, wb_mm))
                    if fcg >= 2:
                        w.append((sas, sas.v - 1))
                    wtsil = [(sas, sas.v - 1)] if fcg >= 2 else []
                    for c in range(NCH):
                        P.op("tensor", lambda e, gb=gb, gs_=gs_, c=c, fc=fc: e.matmul(
                            gb[:], gs_[:, c * 512 + fc * 128:c * 512 + (fc + 1) * 128],
                            xn_s[:, c * 512:(c + 1) * 512],
                            start=(c == 0), stop=(c == NCH - 1)),
                            waits=(w if c == 0 else ()),
                            inc=(swf if (c == NCH - 1 and fc == 3) else None))
                    for c in range(NCH):
                        P.op("tensor", lambda e, ub=ub, us_=us_, c=c, fc=fc: e.matmul(
                            ub[:], us_[:, c * 512 + fc * 128:c * 512 + (fc + 1) * 128],
                            xn_s[:, c * 512:(c + 1) * 512],
                            start=(c == 0), stop=(c == NCH - 1)),
                            inc=(swf if (c == NCH - 1 and fc == 3) else
                                 (te if c == NCH - 1 else None)))
                    gu_done = te.v if not (fc == 3) else te.v
                    wch = [(swf, swf.v)] if fc == 3 else [(te, te.v)]
                    P.op("scalar", lambda e, gb=gb, fcg=fcg: e.activation(
                        tsil[fcg % 2][:], gb[:], AF.Tanh, scale=0.5),
                        waits=wch + wtsil, inc=sc)
                    ts_done = sc.v
                    P.op("vector", lambda e, ub=ub, fcg=fcg: e.scalar_tensor_tensor(
                        v1t[fcg % 2][:], tsil[fcg % 2][:], 1.0, ub[:],
                        ALU.add, ALU.mult), waits=[(sc, ts_done)])
                    P.op("vector", lambda e, gb=gb, fcg=fcg: e.tensor_mul(
                        v1t[fcg % 2][:], v1t[fcg % 2][:], gb[:]))
                    if use_wb:
                        P.op("vector", lambda e, fcg=fcg: e.tensor_mul(
                            act_sb[:, fcg * 512:(fcg + 1) * 512],
                            v1t[fcg % 2][:], wb_ps[:, 0:512]), inc=sas)
                    else:
                        P.op("vector", lambda e, fcg=fcg: e.tensor_scalar(
                            act_sb[:, fcg * 512:(fcg + 1) * 512],
                            v1t[fcg % 2][:], 0.5, None, ALU.mult), inc=sas)
            return sas.v

        def down_phase(nfc, dview, npiece, piece_cols, first_write):
            for pc in range(npiece):
                ds_, dlc = wload(dview(pc))
                nds = piece_cols // 512
                for tt in range(4):
                    for dsl in range(nds):
                        gidx = (pc * 4 + tt) * nds + dsl
                        bank = y_ps[(tt * nds + dsl) % 2]
                        w = [(swl, dlc), (sas, act_target)]
                        if True:
                            w.append((sra, max(0, sra.v - 1)))
                        for c in range(nfc):
                            P.op("tensor", lambda e, bank=bank, ds_=ds_, c=c, tt=tt,
                                 dsl=dsl, piece_cols=piece_cols: e.matmul(
                                     bank[:],
                                     act_sb[:, c * 512 + tt * 128:c * 512 + (tt + 1) * 128],
                                     ds_[:, c * piece_cols + dsl * 512:
                                         c * piece_cols + (dsl + 1) * 512],
                                     start=(c == 0), stop=(c == nfc - 1)),
                                 waits=(w if c == 0 else ()),
                                 inc=(swf if (c == nfc - 1 and tt == 3 and dsl == nds - 1)
                                      else (te if c == nfc - 1 else None)))
                        dmm = swf.v if (tt == 3 and dsl == nds - 1) else te.v
                        col0 = tt * DM + pc * piece_cols + dsl * 512
                        wsem = (swf, dmm) if (tt == 3 and dsl == nds - 1) else (te, dmm)
                        if first_write:
                            P.op("vector", lambda e, bank=bank, col0=col0: e.tensor_copy(
                                routed[:, col0:col0 + 512], bank[:]),
                                waits=[wsem], inc=sra)
                        else:
                            P.op("vector", lambda e, bank=bank, col0=col0: e.tensor_add(
                                routed[:, col0:col0 + 512], routed[:, col0:col0 + 512],
                                bank[:]), waits=[wsem], inc=sra)

        for e_idx in range(NE):
            if ss == 0 and e_idx % 4 == 0:
                cblk = e_idx // 4
                P.op("gpsimd", lambda e, cblk=cblk: e.collective_compute(
                    "ReduceScatter", ALU.add, replica_groups=[list(range(8))],
                    ins=[y_part.ap()[cblk * 2048:(cblk + 1) * 2048, :].opt()],
                    outs=[y_own.ap()[cblk * 256:(cblk + 1) * 256, :].opt()]),
                    waits=[rs_ready], inc=cc2)
            P.op("sync", lambda e, e_idx=e_idx: e.dma_start(
                out=sel_sb[:], in_=selc[:, e_idx * 128:(e_idx + 1) * 128]),
                waits=[(te, te.v)], inc=ssel, inc_n=16)
            sel_ld = ssel.v
            P.op("tensor", lambda e: e.matmul(
                wb_ps[:, 0:512], sel_sb[:], Wfm[:], start=True, stop=True),
                waits=[(vv, wfm_done), (sas, sas.v), (ssel, sel_ld)], inc=te)
            wb_mm = te.v
            gv = lambda hh, e_idx=e_idx: gateT[e_idx].rearrange(
                "(c p) f -> p c f", p=128)[:, :, hh * 512:(hh + 1) * 512]
            uv = lambda hh, e_idx=e_idx: upT[e_idx].rearrange(
                "(c p) f -> p c f", p=128)[:, :, hh * 512:(hh + 1) * 512]
            act_target = None
            act_target = gu_act(2, gv, uv, True, e_idx)
            dv = lambda pc, e_idx=e_idx: downT[e_idx].rearrange(
                "(c p) f -> p c f", p=128)[:, :, pc * 1024:(pc + 1) * 1024]
            down_phase(8, dv, 2, 1024, first_write=(e_idx == 0))
        gvs = lambda hh: shgT.rearrange("(c p) f -> p c f", p=128)[
            :, :, hh * 512:(hh + 1) * 512]
        uvs = lambda hh: shuT.rearrange("(c p) f -> p c f", p=128)[
            :, :, hh * 512:(hh + 1) * 512]
        act_target = gu_act(4, gvs, uvs, False, 0)
        dvs = lambda pc: shdT.rearrange("(c p) f -> p c f", p=128)[
            :, :, pc * 512:(pc + 1) * 512]
        down_phase(16, dvs, 4, 512, first_write=False)

        sil_cons = []
        for tt in range(4):
            for cc in range(4):
                gci = tt * 4 + cc
                wt_ = [(sra, sra.v)] if gci == 0 else []
                if gci >= 2:
                    wt_.append((vv, sil_cons[gci - 2]))
                P.op("scalar", lambda e, tt=tt, cc=cc: e.activation(
                    tsil[cc % 2][:], routed[:, tt * DM + cc * 512:tt * DM + (cc + 1) * 512],
                    AF.Tanh, scale=0.5), waits=wt_, inc=sc)
                gs_done = sc.v
                w = [(sc, gs_done)]
                if tt >= 1 and cc == 0:
                    w.append((sgd, sgd.v))
                P.op("vector", lambda e, tt=tt, cc=cc: e.scalar_tensor_tensor(
                    v1t[cc % 2][:], tsil[cc % 2][:], 1.0,
                    routed[:, tt * DM + cc * 512:tt * DM + (cc + 1) * 512],
                    ALU.add, ALU.mult), waits=w)
                P.op("vector", lambda e, cc=cc: e.tensor_scalar(
                    gstage[:, cc * 512:(cc + 1) * 512], v1t[cc % 2][:],
                    0.5, None, ALU.mult), inc=vv)
                sil_cons.append(vv.v)
            gsv_done = vv.v
            P.op("sync", lambda e, tt=tt, ss=ss: e.dma_start(
                out=g_dram.ap()[ss * 512 + tt * 128:ss * 512 + (tt + 1) * 128, :],
                in_=gstage[:]), waits=[(vv, gsv_done)], inc=sgd, inc_n=16)
        if DEBUG_TAPS and ss == 0:
            P.op("sync", lambda e: e.dma_start(out=dbg["d_W"], in_=Wt[:]),
                 waits=[(pp, W_done)], inc=ld, inc_n=16)
            P.op("sync", lambda e: e.dma_start(out=dbg["d_routed"], in_=routed[:]),
                 waits=[(sra, sra.v)], inc=ld, inc_n=16)
            P.op("sync", lambda e: e.dma_start(out=dbg["d_act"], in_=act_sb[:]),
                 waits=[(sra, sra.v)], inc=ld, inc_n=16)
        strip_done.append(sgd.v)
    ffn_end = [(sgd, strip_done[-1]), (sra, sra.v)]
    F2.close()

    F4 = ExitStack()
    qsb = lambda n, s, d: F4.enter_context(nc.sbuf_tensor(n, s, d))
    x4 = [qsb(f"x4_{i}", [128, DM], F32) for i in range(2)]
    g4 = [qsb(f"g4_{i}", [128, DM], F32) for i in range(2)]
    y4 = [qsb(f"y4_{i}", [128, DM], F32R) for i in range(2)]
    t4 = qsb("t4", [128, DM], F32)
    s4 = qsb("s4", [128, DM], F32)
    o4 = [qsb(f"o4_{i}", [128, DM], F32) for i in range(2)]
    sout = P.sem("sout")
    sv4 = P.sem("sv4")

    for tt in range(NTILE):
        w = list(ffn_end) + [(cc2, 4)]
        if tt >= 2:
            w.append((sv4, sv4.v - 1))
        P.op("sync", lambda e, tt=tt: e.dma_start(
            out=x4[tt % 2][:], in_=x_own[tt * 128:(tt + 1) * 128, :]),
            waits=w, inc=ld, inc_n=16)
        P.op("sync", lambda e, tt=tt: e.dma_start(
            out=g4[tt % 2][:], in_=g_dram.ap()[tt * 128:(tt + 1) * 128, :]),
            inc=ld, inc_n=16)
        P.op("sync", lambda e, tt=tt: e.dma_start(
            out=y4[tt % 2][:], in_=y_own.ap()[tt * 128:(tt + 1) * 128, :]),
            inc=ld, inc_n=16)
        t_ld = ld.v
        P.op("vector", lambda e, tt=tt: e.tensor_scalar(
            t4[:], x4[tt % 2][:], mcol[:, tt:tt + 1], rcol[:, tt:tt + 1],
            ALU.subtract, ALU.mult), waits=[(ld, t_ld)])
        P.op("vector", lambda e, tt=tt: e.tensor_add(s4[:], t4[:], y4[tt % 2][:]))
        P.op("vector", lambda e, tt=tt: e.tensor_mul(s4[:], s4[:], g4[tt % 2][:]))
        w2 = []
        if tt >= 2:
            w2.append((sout, 16 * (tt - 1)))
        P.op("vector", lambda e, tt=tt: e.tensor_add(
            o4[tt % 2][:], s4[:], x4[tt % 2][:]), waits=w2, inc=sv4)
        oready = sv4.v
        P.op("sync", lambda e, tt=tt: e.dma_start(
            out=out[tt * 128:(tt + 1) * 128, :], in_=o4[tt % 2][:]),
            waits=[(sv4, oready)], inc=sout, inc_n=16)

    if DEBUG_TAPS:
        P.op("gpsimd", lambda e: e.dma_start(out=dbg["d_g"], in_=g_dram.ap()),
             waits=ffn_end, inc=gl, inc_n=16)
        P.op("gpsimd", None, waits=[(gl, gl.v)])
    P.op("sync", None, waits=[(sout, 16 * NTILE), (ld, ld.v)])
    F4.close()

    with ExitStack() as EX:
        blk = EX.enter_context(nc.Block())
        P.emit(nc, blk, EX)
    CX.close()
    return nc


_NC_CACHE = {}


def _host_prep(inputs):
    w_in = np.asarray(inputs["L0_attn_w_in"], np.float32)
    rms_w = np.asarray(inputs["L0_attn_rms_w"], np.float32)
    w_out = np.asarray(inputs["L0_attn_w_out"], np.float32)
    gate_weight = np.asarray(inputs["L0_ffn_gate_weight"], np.float32)
    e_bias = np.asarray(inputs["L0_ffn_e_score_correction_bias"], np.float32)
    gate_w = np.asarray(inputs["L0_ffn_gate_w"], np.float32)
    up_w = np.asarray(inputs["L0_ffn_up_w"], np.float32)
    down_w = np.asarray(inputs["L0_ffn_down_w"], np.float32)
    sh_g = np.asarray(inputs["L0_ffn_sh_gate_w"], np.float32)
    sh_u = np.asarray(inputs["L0_ffn_sh_up_w"], np.float32)
    sh_d = np.asarray(inputs["L0_ffn_sh_down_w"], np.float32)
    x = np.asarray(inputs["L0_attn_x"], np.float32).reshape(S, DM)

    inv = (10000.0 ** (-(np.arange(32, dtype=np.float64) / 32.0))).astype(np.float64)
    ang = np.arange(S, dtype=np.float64)[:, None] * inv[None, :]
    cosT = np.cos(ang).T.astype(np.float32)
    sinT = np.sin(ang).T.astype(np.float32)
    ropeq_h = np.concatenate([np.concatenate([cosT, cosT], 0),
                              np.concatenate([sinT, sinT], 0)], 0) / 8.0
    ropeq_h = ropeq_h.astype(np.float32)
    ropek_h = np.concatenate([cosT, sinT], 0).astype(np.float32)

    ident = np.eye(128, dtype=np.float32)
    dm_ = np.zeros((128, 4, STRIP), np.float32)
    for d in range(4):
        for k in range(128):
            dm_[k, d, :] = (128 * d + k <= np.arange(STRIP)).astype(np.float32)
    dmasks_h = dm_.reshape(128, 4 * STRIP)

    sel = np.zeros((NE, NE * 128), np.float32)
    for e in range(NE):
        sel[e, e * 128:(e + 1) * 128] = 1.0

    common = {
        "rmsw_row": np.ascontiguousarray(np.broadcast_to(rms_w, (128, DM))),
        "ropeq": ropeq_h, "ropek": ropek_h,
        "ident_r": ident, "ident_f": ident,
        "dmasks": dmasks_h,
        "gwT": np.ascontiguousarray(gate_weight.T),
        "ebias": np.ascontiguousarray(np.broadcast_to(e_bias, (128, NE))),
        "selc": sel,
        "gateT": np.ascontiguousarray(gate_w.transpose(0, 2, 1)),
        "upT": np.ascontiguousarray(up_w.transpose(0, 2, 1)),
        "downT": np.ascontiguousarray(down_w.transpose(0, 2, 1)),
        "shgT": np.ascontiguousarray(sh_g.T),
        "shuT": np.ascontiguousarray(sh_u.T),
        "shdT": np.ascontiguousarray(sh_d.T),
    }
    in_maps = []
    for g in range(8):
        heads = [4 * g + j for j in range(HPG)]
        rows = []
        for pair in range(2):
            hA, hB = heads[2 * pair], heads[2 * pair + 1]
            rows += [hA * 64 + d_ for d_ in range(32)]
            rows += [hB * 64 + d_ for d_ in range(32)]
            rows += [hA * 64 + 32 + d_ for d_ in range(32)]
            rows += [hB * 64 + 32 + d_ for d_ in range(32)]
        rows += [2048 + 64 * g + d_ for d_ in range(64)]
        rows += [2560 + 64 * g + d_ for d_ in range(64)]
        w_in_g = np.ascontiguousarray(w_in[rows, :].T)
        w_og_g = np.ascontiguousarray(w_out.T[g * 256:(g + 1) * 256, :])
        m = dict(common)
        m["x_own"] = np.ascontiguousarray(x[g * TOK:(g + 1) * TOK, :])
        m["w_in_gT"] = w_in_g
        m["w_og"] = w_og_g
        in_maps.append(m)
    return in_maps


def kernel(**inputs):
    in_maps = _host_prep(inputs)
    nc = build_program()
    res = run_bass_kernel_spmd(nc, in_maps, core_ids=list(range(8)))
    parts = [np.asarray(res.results[i]["out"], np.float32) for i in range(8)]
    return np.concatenate(parts, axis=0).reshape(1, S, DM)


if __name__ == "__main__":
    nc = build_program()
    print("built ok, n_instructions:", nc.n_instructions())
